# version 03
# baseline (speedup 1.0000x reference)
import math
import numpy as np
import ml_dtypes
import concourse.bass as bass
import concourse.mybir as mybir
from concourse.bass_utils import run_bass_kernel_spmd

F32 = mybir.dt.float32
BF16 = mybir.dt.bfloat16
AF = mybir.ActivationFunctionType
ALU = mybir.AluOpType
AX = mybir.AxisListType
BF = ml_dtypes.bfloat16

B, T, D = 4, 2048, 2048
H, DK, DV = 16, 128, 128
HG = 8
KCH = 16
C = 128
PASS_T = 1024
NPASS = 2
CPP = PASS_T // C
INTER = 5632
MI = INTER // 128
FILL = -30000.0
LNEPS = 1e-5
EPS6 = 1e-6
LNDK = -0.5 * math.log(DK)


class Prog:

    ENGS = ("sync", "gpsimd", "vector", "scalar", "tensor")

    def __init__(self, nc):
        self.nc = nc
        self.ops = []

    def add(self, eng, emit, r=(), w=(), dma=False, special=False):
        if special:
            r = tuple(r) + ("warm",)
        self.ops.append((eng, emit, tuple(r), tuple(w), bool(dma), bool(special)))

    def finalize(self, block):
        n = len(self.ops)
        deps = [set() for _ in range(n)]
        last_w = {}
        readers = {}
        for i, (eng, emit, reads, writes, dma, special) in enumerate(self.ops):
            for b in reads:
                if b in last_w:
                    deps[i].add(last_w[b])
            for b in writes:
                if b in last_w:
                    deps[i].add(last_w[b])
                for rd in readers.get(b, ()):
                    deps[i].add(rd)
            for b in reads:
                readers.setdefault(b, []).append(i)
            for b in writes:
                last_w[b] = i
                readers[b] = []
        UNSAFE = ("vector", "gpsimd")
        need = [False] * n
        for i in range(n):
            eng, _, _, _, dma, special = self.ops[i]
            if dma:
                need[i] = True
            for d in deps[i]:
                if (self.ops[d][4] or self.ops[d][0] != eng or special
                        or self.ops[d][5] or eng in UNSAFE):
                    need[d] = True
        sems = {}
        for e in self.ENGS:
            sems[(e, False)] = self.nc.alloc_semaphore(f"sb_{e}_c")
            sems[(e, True)] = self.nc.alloc_semaphore(f"sb_{e}_d")
        counter = {k: 0 for k in sems}
        val = [None] * n
        for i in range(n):
            if need[i]:
                k = (self.ops[i][0], self.ops[i][4])
                counter[k] += 16 if self.ops[i][4] else 1
                val[i] = (k, counter[k])
        by_eng = {e: [] for e in self.ENGS}
        for i, op in enumerate(self.ops):
            by_eng[op[0]].append(i)

        def mk_section(eng_name, idxs):
            def section(eng):
                wm = {}
                for i in idxs:
                    _, emit, _, _, dma, special = self.ops[i]
                    waits = {}
                    for d in deps[i]:
                        d_eng, d_dma, d_sp = (self.ops[d][0], self.ops[d][4],
                                              self.ops[d][5])
                        if (not d_dma and d_eng == eng_name
                                and not (special or d_sp)
                                and eng_name not in ("vector", "gpsimd")):
                            continue
                        k, v = val[d]
                        if waits.get(k, 0) < v:
                            waits[k] = v
                    for k in sorted(waits, key=str):
                        v = waits[k]
                        if wm.get(k, 0) < v:
                            eng.wait_ge(sems[k], v)
                            wm[k] = v
                    inst = emit(eng)
                    if need[i]:
                        k, v = val[i]
                        inst.then_inc(sems[k], 16 if k[1] else 1)
            return section

        for e in self.ENGS:
            if by_eng[e]:
                getattr(block, e)(mk_section(e, by_eng[e]))
        finals = [(k, v) for k, v in counter.items() if v > 0]

        def tail(eng):
            for k, v in sorted(finals, key=lambda kv: str(kv[0])):
                eng.wait_ge(sems[k], v)
        block.sync(tail)


def build_attn(dbg=()):
    nc = bass.Bass("TRN2", target_bir_lowering=False, debug=False,
                   detect_race_conditions=False)
    p = Prog(nc)

    x_d = nc.dram_tensor("x", [T, D], F32, kind="ExternalInput").ap()
    whead_d = nc.dram_tensor("whead", [D, HG * 512], F32, kind="ExternalInput").ap()
    convw_d = nc.dram_tensor("convw", [HG * 3 * 128, 4], F32, kind="ExternalInput").ap()
    wab_d = nc.dram_tensor("wab", [D, 16], F32, kind="ExternalInput").ap()
    spsign_d = nc.dram_tensor("spsign", [16, 1], F32, kind="ExternalInput").ap()
    spbias_d = nc.dram_tensor("spbias", [16, 1], F32, kind="ExternalInput").ap()
    alog_d = nc.dram_tensor("alog", [16, 1], F32, kind="ExternalInput").ap()
    nw_d = nc.dram_tensor("nw", [128, 128], F32, kind="ExternalInput").ap()
    identf_d = nc.dram_tensor("identf", [128, 128], F32, kind="ExternalInput").ap()
    ibf_d = nc.dram_tensor("ibf", [128, 128], BF16, kind="ExternalInput").ap()
    onesf_d = nc.dram_tensor("onesf", [128, 128], F32, kind="ExternalInput").ap()
    e127_d = nc.dram_tensor("e127", [128, 128], F32, kind="ExternalInput").ap()
    epsc_d = nc.dram_tensor("epsc", [128, 4], F32, kind="ExternalInput").ap()
    maskm_d = nc.dram_tensor("maskm", [128, 3 * 128], F32, kind="ExternalInput").ap()
    maska_d = nc.dram_tensor("maska", [128, 3 * 128], F32, kind="ExternalInput").ap()

    onrm_d = nc.dram_tensor("onrm", [HG, 128, T], BF16, kind="ExternalOutput").ap()
    dbg_d = {}
    for nm, shp, dt in [
        ("xn", [128, KCH * PASS_T], BF16), ("qkv", [4, 128, PASS_T], BF16),
        ("sc", [16, PASS_T], F32), ("scB", [16, PASS_T], F32), ("ktm", [128, CPP * 128], BF16),
        ("TT", [128, 128], BF16), ("U", [128, 128], BF16),
        ("S", [128, 128], F32), ("AL", [128, 128], BF16),
        ("expin", [128, 3 * 128], F32), ("hcols", [128, 14], F32),
    ]:
        if nm in dbg:
            dbg_d[nm] = nc.dram_tensor("dbg_" + nm, shp, dt,
                                       kind="ExternalOutput").ap()

    Al = nc.alloc_sbuf_tensor
    xn = Al("xn", [128, KCH, PASS_T], BF16)
    xld = [Al(f"xld{i}", [128, D], F32) for i in range(2)]
    xtm = Al("xtm", [128, D], BF16)
    lnc_ = Al("lncols", [128, 4], F32)
    bst = Al("bnst", [128, 4, 6], F32)
    wbuf = [Al(f"wbuf{i}", [128, KCH, 512], BF16) for i in range(2)]
    convw = Al("convwt", [128, HG * 3, 4], F32)
    mixs = [Al(f"mixs{i}", [128, 1027], F32) for i in range(2)]
    cvac = Al("cvac", [128, 1024], F32)
    tails = Al("tails", [128, HG * 3, 3], F32)
    qf = Al("qf", [128, PASS_T], BF16)
    kf = Al("kf", [128, PASS_T], BF16)
    vf = Al("vf", [128, PASS_T], BF16)
    zst = Al("zst", [128, PASS_T], BF16)
    qtm = Al("qtm", [128, CPP, 128], BF16)
    ktm = Al("ktm", [128, CPP, 128], BF16)
    vtm = Al("vtm", [128, CPP, 128], BF16)
    ztm = Al("ztm", [128, HG, CPP, 128], BF16)
    ogtm = Al("ogtm", [128, CPP, 128], BF16)
    ofm = Al("ofm", [128, CPP, 128], BF16)
    scG = Al("scG", [16, PASS_T], F32)
    scf = Al("scf", [16, 2, PASS_T], F32)
    scpack = Al("scpack", [128, CPP, 32], F32)
    hcols = Al("hcols", [128, 14, CPP], F32)
    rcol = Al("rcol", [128, 2], F32)
    colbc = Al("colbc", [128, 128], F32)
    expin = Al("expin", [128, 3, 128], F32)
    expout = Al("expout", [128, 3, 128], BF16)
    AL_ = Al("tAL", [128, 128], BF16)
    AU_ = Al("tAU", [128, 128], BF16)
    AUn_ = Al("tAUn", [128, 128], BF16)
    MT_ = Al("tMT", [128, 128], BF16)
    pw = {}
    for lv in (2, 4, 8, 16, 32):
        pw[(lv, "L")] = Al(f"A{lv}L", [128, 128], BF16)
        pw[(lv, "U")] = Al(f"A{lv}U", [128, 128], BF16)
    pw[(64, "L")] = Al("A64L", [128, 128], BF16)
    X_ = Al("Xtt", [128, 128], BF16)
    Rv = Al("Rv", [128, 128], BF16)
    Rk = Al("Rk", [128, 128], BF16)
    Kt_ = Al("Kt", [128, 128], BF16)
    Wfm = Al("Wfm", [128, 128], BF16)
    Utm = Al("Utm", [128, 128], BF16)
    S_f = Al("S_f", [128, HG, 128], F32)
    S_b = Al("S_b", [128, 128], BF16)
    gts = Al("gts", [128, 128], F32)
    sq2 = Al("sq2", [128, CPP, 128], F32)
    cnw = Al("c_nw", [128, 128], F32)
    cidf = Al("c_identf", [128, 128], F32)
    cibf = Al("c_ibf", [128, 128], BF16)
    cones = Al("c_onesf", [128, 128], F32)
    ce127 = Al("c_e127", [128, 128], F32)
    cepsc = Al("c_epsc", [128, 4], F32)
    cmaskm = Al("c_maskm", [128, 3, 128], F32)
    cmaska = Al("c_maska", [128, 3, 128], F32)
    csgn = Al("c_spsign", [16, 1], F32)
    cbia = Al("c_spbias", [16, 1], F32)
    calog = Al("c_alog", [16, 1], F32)
    negal = Al("negal", [16, 1], F32)
    scB = Al("scB", [16, PASS_T], F32)

    ps = [nc.alloc_psum_tensor(f"ps{i}", [128, 512], F32) for i in range(8)]

    def dbg_store(nm, src, rkeys):
        if nm in dbg_d:
            p.add("sync", lambda e, nm=nm, src=src: e.dma_start(
                out=dbg_d[nm], in_=src), r=rkeys, dma=True)

    for dst, key, dr in [(cnw, "c_nw", nw_d), (cidf, "c_idf", identf_d),
                         (cibf, "c_ibf", ibf_d), (cones, "c_ones", onesf_d),
                         (ce127, "c_e127", e127_d), (csgn, "c_sgn", spsign_d),
                         (cbia, "c_bia", spbias_d),
                         (cepsc, "c_epsc", epsc_d),
                         (cmaskm, "c_maskm", maskm_d),
                         (cmaska, "c_maska", maska_d)]:
        p.add("sync", lambda e, dst=dst, dr=dr: e.dma_start(
            out=dst[:] if len(dst.shape) == 2 else dst[:].rearrange("p a b -> p (a b)"),
            in_=dr), w=[key], dma=True)
    p.add("sync", lambda e: e.dma_start(out=calog[:], in_=alog_d),
          w=["c_alog"], dma=True)
    p.add("sync", lambda e: e.dma_start(
        out=convw[:], in_=convw_d.rearrange("(g q) k -> q g k", q=128)),
        w=["convw"], dma=True)
    p.add("scalar", lambda e: e.activation(negal[:], calog[:], AF.Exp),
          r=["c_alog"], w=["negal"])
    p.add("vector", lambda e: e.tensor_scalar(negal[:], negal[:], csgn[:, 0:1],
                                              -1.0, ALU.mult, ALU.mult),
          r=["negal", "c_sgn"], w=["negal"])
    p.add("vector", lambda e: e.memset(tails[:], 0.0), w=["tails"])
    p.add("vector", lambda e: e.memset(S_f[:], 0.0), w=["S_f"])
    p.add("vector", lambda e: e.memset(bst[:], 0.0), w=["bst"])
    p.add("vector", lambda e: e.bn_stats(bst[:, 0, :], bst[:, 2, 0:6]),
          r=["bst"], w=["bst", "warm"])
    p.add("vector", lambda e: e.bn_aggr(bst[:, 1, 0:2], bst[:, 0, :]),
          r=["bst"], w=["bst", "warm"])
    p.add("vector", lambda e: e.tensor_tensor_scan(
        bst[:, 2, 0:6], bst[:, 3, 0:6], bst[:, 2, 0:6], 0.0,
        ALU.mult, ALU.add),
        r=["bst"], w=["bst", "warm"])

    def emit_instance(ps_i, hh, ck):
        first = (ps_i == 0 and ck == 0)
        dbg_on = ("TT" in dbg_d and ps_i == 0 and hh == 0 and ck == 1)
        for v, (growcol, subcol, neg) in enumerate(
                [(5, 4, True), (4, 5, False), (6, 5, False)]):
            p.add("vector", lambda e, gc=growcol: e.tensor_scalar(
                colbc[:], cones[:], hcols[:, gc, ck:ck + 1], None, ALU.mult),
                r=["c_ones", "hcols"], w=["colbc"])
            p.add("tensor", lambda e, v=v: e.transpose(
                ps[1][:, v * 128:(v + 1) * 128], colbc[:], cidf[:]),
                r=["colbc", "c_idf"], w=["ps1"])
            if neg:
                p.add("vector", lambda e, v=v, sc=subcol: e.tensor_scalar(
                    expin[:, v, :], ps[1][:, v * 128:(v + 1) * 128],
                    hcols[:, sc, ck:ck + 1], -1.0, ALU.subtract, ALU.mult),
                    r=["ps1", "hcols"], w=["expin"])
            else:
                p.add("vector", lambda e, v=v, sc=subcol: e.tensor_scalar(
                    expin[:, v, :], ps[1][:, v * 128:(v + 1) * 128],
                    hcols[:, sc, ck:ck + 1], None, ALU.subtract),
                    r=["ps1", "hcols"], w=["expin"])
        p.add("gpsimd", lambda e: e.tensor_mul(
            expin[:].rearrange("p a b -> p (a b)"),
            expin[:].rearrange("p a b -> p (a b)"),
            cmaskm[:].rearrange("p a b -> p (a b)")),
            r=["expin", "c_maskm"], w=["expin"])
        p.add("gpsimd", lambda e: e.tensor_add(
            expin[:].rearrange("p a b -> p (a b)"),
            expin[:].rearrange("p a b -> p (a b)"),
            cmaska[:].rearrange("p a b -> p (a b)")),
            r=["expin", "c_maska"], w=["expin"])
        if "expin" in dbg_d and dbg_on:
            dbg_store("expin", expin[:].rearrange("p a b -> p (a b)"), ["expin"])
            p.add("sync", lambda e: e.dma_start(
                out=dbg_d["hcols"],
                in_=hcols[:, :, ck:ck + 1].rearrange("p a b -> p (a b)"),
                allow_slow_non_contiguous=True),
                r=["hcols"], dma=True) if "hcols" in dbg_d else None
        p.add("scalar", lambda e: e.activation(
            expout[:].rearrange("p a b -> p (a b)"),
            expin[:].rearrange("p a b -> p (a b)"), AF.Exp),
            r=["expin"], w=["expout"])
        p.add("tensor", lambda e: e.matmul(ps[0][:, 0:128], kf[:, ck * 128:(ck + 1) * 128],
                                           kf[:, ck * 128:(ck + 1) * 128],
                                           start=True, stop=True),
              r=["kf"], w=["ps0"])
        p.add("tensor", lambda e: e.matmul(ps[0][:, 128:256], kf[:, ck * 128:(ck + 1) * 128],
                                           qf[:, ck * 128:(ck + 1) * 128],
                                           start=True, stop=True),
              r=["kf", "qf"], w=["ps0"])
        p.add("vector", lambda e: e.tensor_mul(AL_[:], ps[0][:, 0:128],
                                               expout[:, 0, :]),
              r=["ps0", "expout"], w=["AL"])
        p.add("vector", lambda e: e.tensor_mul(AU_[:], ps[0][:, 0:128],
                                               expout[:, 1, :]),
              r=["ps0", "expout"], w=["AU"])
        p.add("vector", lambda e: e.tensor_scalar_mul(AUn_[:], AU_[:], -1.0),
              r=["AU"], w=["AUn"])
        p.add("vector", lambda e: e.tensor_mul(MT_[:], ps[0][:, 128:256],
                                               expout[:, 2, :]),
              r=["ps0", "expout"], w=["MT"])
        if "AL" in dbg_d and dbg_on:
            dbg_store("AL", AL_[:], ["AL"])
        chain = [((2, "L"), AU_, AL_), ((2, "U"), AL_, AU_)]
        for lv in (4, 8, 16, 32):
            chain.append(((lv, "L"), pw[(lv // 2, "U")], pw[(lv // 2, "L")]))
            chain.append(((lv, "U"), pw[(lv // 2, "L")], pw[(lv // 2, "U")]))
        chain.append(((64, "L"), pw[(32, "U")], pw[(32, "L")]))
        for idx, (key, lhs, rhs) in enumerate(chain):
            bank = 4 + idx % 2
            bkey = f"ps{bank}"
            def lname(t):
                if t is AU_:
                    return "AU"
                if t is AL_:
                    return "AL"
                for kk, vv in pw.items():
                    if vv is t:
                        return f"A{kk[0]}{kk[1]}"
                raise KeyError
            p.add("tensor", lambda e, bank=bank, lhs=lhs, rhs=rhs: e.matmul(
                ps[bank][:, 0:128], lhs[:], rhs[:], start=True, stop=True),
                r=[lname(lhs), lname(rhs)], w=[bkey])
            p.add("scalar", lambda e, bank=bank, key=key: e.copy(
                pw[key][:], ps[bank][:, 0:128]),
                r=[bkey], w=[f"A{key[0]}{key[1]}"])
        p.add("vector", lambda e: e.tensor_add(X_[:], cibf[:], AUn_[:]),
              r=["c_ibf", "AUn"], w=["X"])
        for li, lv in enumerate((2, 4, 8, 16, 32, 64)):
            bank = 4 + li % 2
            bkey = f"ps{bank}"
            p.add("tensor", lambda e, bank=bank, lv=lv: e.matmul(
                ps[bank][:, 0:128], pw[(lv, "L")][:], X_[:],
                start=True, stop=False),
                r=[f"A{lv}L", "X"], w=[bkey])
            p.add("tensor", lambda e, bank=bank: e.matmul(
                ps[bank][:, 0:128], cibf[:], X_[:], start=False, stop=True),
                r=["c_ibf", "X"], w=[bkey])
            p.add("scalar", lambda e, bank=bank: e.copy(X_[:], ps[bank][:, 0:128]),
                  r=[bkey], w=["X"])
        if "TT" in dbg_d and dbg_on:
            dbg_store("TT", X_[:], ["X"])
        p.add("vector", lambda e: e.tensor_scalar(
            Rv[:], vtm[:, ck, :], hcols[:, 7, ck:ck + 1], None, ALU.mult),
            r=["vtm", "hcols"], w=["Rv"])
        p.add("vector", lambda e: e.tensor_scalar(
            Rk[:], ktm[:, ck, :], hcols[:, 8, ck:ck + 1], None, ALU.mult),
            r=["ktm", "hcols"], w=["Rk"])
        p.add("vector", lambda e: e.tensor_scalar(
            Kt_[:], ktm[:, ck, :], hcols[:, 9, ck:ck + 1], None, ALU.mult),
            r=["ktm", "hcols"], w=["Kt"])
        p.add("scalar", lambda e: e.copy(S_b[:], S_f[:, hh, :]),
              r=["S_f"], w=["S_b"])
        p.add("tensor", lambda e: e.matmul(ps[6][:, 0:128], Rk[:], X_[:],
                                           start=True, stop=True),
              r=["Rk", "X"], w=["ps6"])
        p.add("scalar", lambda e: e.copy(Wfm[:], ps[6][:, 0:128]),
              r=["ps6"], w=["Wfm"])
        p.add("tensor", lambda e: e.matmul(ps[6][:, 128:256], X_[:], Rv[:],
                                           start=True, stop=False),
              r=["X", "Rv"], w=["ps6"])
        p.add("tensor", lambda e: e.matmul(ps[6][:, 128:256], Wfm[:], S_b[:],
                                           start=False, stop=True),
              r=["Wfm", "S_b"], w=["ps6"])
        p.add("vector", lambda e: e.tensor_copy(Utm[:], ps[6][:, 128:256]),
              r=["ps6"], w=["Utm"])
        if "U" in dbg_d and dbg_on:
            dbg_store("U", Utm[:], ["Utm"])
        p.add("tensor", lambda e: e.matmul(ps[7][:, 0:128],
                                           qf[:, ck * 128:(ck + 1) * 128], S_b[:],
                                           start=True, stop=True),
              r=["qf", "S_b"], w=["ps7"])
        p.add("tensor", lambda e: e.matmul(ps[7][:, 128:256], MT_[:], Utm[:],
                                           start=True, stop=True),
              r=["MT", "Utm"], w=["ps7"])
        p.add("tensor", lambda e: e.matmul(ps[6][:, 256:384], Kt_[:], Utm[:],
                                           start=True, stop=True),
              r=["Kt", "Utm"], w=["ps6"])
        p.add("vector", lambda e: e.scalar_tensor_tensor(
            S_f[:, hh, :], S_f[:, hh, :], hcols[:, 11, ck:ck + 1], ps[6][:, 256:384],
            ALU.mult, ALU.add),
            r=["S_f", "hcols", "ps6"], w=["S_f"])
        if "S" in dbg_d and ps_i == 0 and hh == 0 and ck == 1:
            dbg_store("S", S_f[:, 0, :], ["S_f"])
        p.add("scalar", lambda e: e.activation(gts[:], ps[7][:, 0:128],
                                               AF.Copy, scale=hcols[:, 12, ck:ck + 1]),
              r=["ps7", "hcols"], w=["gts"])
        p.add("vector", lambda e: e.tensor_add(gts[:], gts[:], ps[7][:, 128:256]),
              r=["gts", "ps7"], w=["gts"])
        p.add("vector", lambda e: e.tensor_mul(gts[:], gts[:], ztm[:, hh, ck, :]),
              r=["gts", "ztm"], w=["gts"])
        p.add("vector", lambda e: e.tensor_mul(sq2[:, 0, :], gts[:], gts[:]),
              r=["gts"], w=["sq2"])
        p.add("vector", lambda e: e.reduce_sum(rcol[:, 0:1], sq2[:, 0, :], axis=AX.X),
              r=["sq2"], w=["rcol"])
        p.add("vector", lambda e: e.tensor_scalar(
            rcol[:, 0:1], rcol[:, 0:1], 1.0 / DV, EPS6, ALU.mult, ALU.add),
            r=["rcol"], w=["rcol"])
        p.add("scalar", lambda e: e.activation(rcol[:, 0:1], rcol[:, 0:1],
                                               AF.Ln),
              r=["rcol"], w=["rcol"])
        p.add("scalar", lambda e: e.activation(rcol[:, 0:1], rcol[:, 0:1],
                                               AF.Exp, scale=-0.5),
              r=["rcol"], w=["rcol"])
        p.add("vector", lambda e, ck=ck: e.scalar_tensor_tensor(
            ogtm[:, ck, :], gts[:], rcol[:, 0:1], cnw[:], ALU.mult, ALU.mult),
            r=["gts", "rcol", "c_nw"], w=["ogtm"])
        if "orow" in dbg_d and dbg_on:
            pass

    def emit_head(ps_i, hh):
        t0 = ps_i * PASS_T
        wb = wbuf[hh % 2]
        wkey = f"wbuf{hh % 2}"
        p.add("gpsimd", lambda e: e.dma_start(
            out=wb[:],
            in_=whead_d.rearrange("(k q) c -> q k c", q=128)[:, :, hh * 512:(hh + 1) * 512]),
            w=[wkey], dma=True)
        outb = {0: qf, 1: kf, 2: vf, 3: zst}
        okey = {0: "qf", 1: "kf", 2: "vf", 3: "zst"}
        for m in range(4):
            mx = mixs[m % 2]
            mxkey = f"mixs{m % 2}"
            for n_ in range(2):
                bank = 2 + n_
                bkey = f"ps{bank}"
                for k in range(KCH):
                    p.add("tensor", lambda e, m=m, n_=n_, k=k, bank=bank: e.matmul(
                        ps[bank][:, 0:512], wb[:, k, m * 128:(m + 1) * 128],
                        xn[:, k, n_ * 512:(n_ + 1) * 512],
                        start=(k == 0), stop=(k == KCH - 1)),
                        r=[wkey, "xn"], w=[bkey])
                if m < 3:
                    p.add("vector", lambda e, n_=n_, bank=bank, mx=mx: e.tensor_copy(
                        mx[:, 3 + n_ * 512: 3 + (n_ + 1) * 512], ps[bank][:, 0:512]),
                        r=[bkey], w=[mxkey])
                else:
                    p.add("scalar", lambda e, n_=n_, bank=bank: e.activation(
                        zst[:, n_ * 512:(n_ + 1) * 512], ps[bank][:, 0:512], AF.Silu),
                        r=[bkey], w=["zst"])
            if m == 3:
                p.add("sync", lambda e: e.dma_start_transpose(
                    ztm[:, hh, :, :], zst[:]),
                    r=["zst"], w=["ztm"], dma=True)
                continue
            gi = hh * 3 + m
            p.add("vector", lambda e, mx=mx, gi=gi: e.tensor_copy(
                mx[:, 0:3], tails[:, gi, :]),
                r=["tails"], w=[mxkey])
            p.add("vector", lambda e, mx=mx, gi=gi: e.tensor_scalar(
                cvac[:], mx[:, 0:1024], convw[:, gi, 0:1], None, ALU.mult),
                r=[mxkey, "convw"], w=["cvac"])
            for j in range(1, 4):
                p.add("vector", lambda e, mx=mx, gi=gi, j=j: e.scalar_tensor_tensor(
                    cvac[:], mx[:, j:j + 1024], convw[:, gi, j:j + 1], cvac[:],
                    ALU.mult, ALU.add),
                    r=[mxkey, "convw", "cvac"], w=["cvac"])
            p.add("vector", lambda e, mx=mx, gi=gi: e.tensor_copy(
                tails[:, gi, :], mx[:, 1024:1027]),
                r=[mxkey], w=["tails"])
            p.add("scalar", lambda e, m=m: e.activation(
                outb[m][:], cvac[:], AF.Silu),
                r=["cvac"], w=[okey[m]])
        if "qkv" in dbg_d and ps_i == 0 and hh == 0:
            for mi_, (buf_, bkey_) in enumerate([(qf, "qf"), (kf, "kf"),
                                                 (vf, "vf"), (zst, "zst")]):
                p.add("sync", lambda e, mi_=mi_, buf_=buf_: e.dma_start(
                    out=dbg_d["qkv"][mi_], in_=buf_[:]), r=[bkey_], dma=True)
        p.add("sync", lambda e: e.dma_start_transpose(qtm[:], qf[:]),
              r=["qf"], w=["qtm"], dma=True)
        p.add("sync", lambda e: e.dma_start_transpose(ktm[:], kf[:]),
              r=["kf"], w=["ktm"], dma=True)
        p.add("sync", lambda e: e.dma_start_transpose(vtm[:], vf[:]),
              r=["vf"], w=["vtm"], dma=True)
        if "ktm" in dbg_d and ps_i == 0 and hh == 0:
            dbg_store("ktm", ktm[:].rearrange("p a b -> p (a b)"), ["ktm"])
        p.add("vector", lambda e, hh=hh: e.tensor_copy(
            hcols[:, 0, :], scpack[:, :, 24 + hh]),
            r=["scpack"], w=["hcols"])
        p.add("vector", lambda e, hh=hh: e.tensor_copy(
            hcols[:, 1, :], scpack[:, :, hh]),
            r=["scpack"], w=["hcols"])
        p.add("gpsimd", lambda e: e.tensor_mul(
            sq2[:].rearrange("p a b -> p (a b)"),
            ktm[:].rearrange("p a b -> p (a b)"),
            ktm[:].rearrange("p a b -> p (a b)")),
            r=["ktm"], w=["sq2"])
        p.add("vector", lambda e: e.reduce_sum(
            hcols[:, 2, :].rearrange("p (a b) -> p a b", b=1), sq2[:], axis=AX.X),
            r=["sq2"], w=["hcols"])
        p.add("gpsimd", lambda e: e.tensor_mul(
            sq2[:].rearrange("p a b -> p (a b)"),
            qtm[:].rearrange("p a b -> p (a b)"),
            qtm[:].rearrange("p a b -> p (a b)")),
            r=["qtm"], w=["sq2"])
        p.add("vector", lambda e: e.reduce_sum(
            hcols[:, 3, :].rearrange("p (a b) -> p a b", b=1), sq2[:], axis=AX.X),
            r=["sq2"], w=["hcols"])
        p.add("scalar", lambda e: e.activation(
            hcols[:, 2, :], hcols[:, 2, :], AF.Ln, bias=cepsc[:, 1:2]),
            r=["hcols", "c_epsc"], w=["hcols"])
        p.add("scalar", lambda e: e.activation(
            hcols[:, 3, :], hcols[:, 3, :], AF.Ln, bias=cepsc[:, 1:2]),
            r=["hcols", "c_epsc"], w=["hcols"])
        p.add("vector", lambda e: e.tensor_sub(
            hcols[:, 4, :], hcols[:, 0, :], hcols[:, 1, :]),
            r=["hcols"], w=["hcols"])
        p.add("vector", lambda e: e.scalar_tensor_tensor(
            hcols[:, 4, :], hcols[:, 2, :], -0.5, hcols[:, 4, :],
            ALU.mult, ALU.add),
            r=["hcols"], w=["hcols"])
        p.add("vector", lambda e: e.scalar_tensor_tensor(
            hcols[:, 5, :], hcols[:, 2, :], 0.5, hcols[:, 0, :],
            ALU.mult, ALU.add),
            r=["hcols"], w=["hcols"])
        p.add("vector", lambda e: e.scalar_tensor_tensor(
            hcols[:, 6, :], hcols[:, 3, :], -0.5, hcols[:, 0, :],
            ALU.mult, ALU.add),
            r=["hcols"], w=["hcols"])
        p.add("vector", lambda e: e.tensor_scalar_add(
            hcols[:, 6, :], hcols[:, 6, :], LNDK),
            r=["hcols"], w=["hcols"])
        p.add("scalar", lambda e: e.activation(
            hcols[:, 7, :], hcols[:, 1, :], AF.Exp, scale=-1.0),
            r=["hcols"], w=["hcols"])
        p.add("scalar", lambda e: e.activation(
            hcols[:, 8, :], hcols[:, 4, :], AF.Exp),
            r=["hcols"], w=["hcols"])
        p.add("vector", lambda e: e.tensor_scalar_mul(
            hcols[:, 8, :], hcols[:, 8, :], -1.0),
            r=["hcols"], w=["hcols"])
        p.add("scalar", lambda e: e.activation(
            hcols[:, 12, :], hcols[:, 6, :], AF.Exp),
            r=["hcols"], w=["hcols"])
        p.add("tensor", lambda e: e.matmul(
            ps[1][:, 384:384 + CPP], ce127[:], hcols[:, 0, :],
            start=True, stop=True),
            r=["c_e127", "hcols"], w=["ps1"])
        p.add("vector", lambda e: e.tensor_copy(
            hcols[:, 10, :], ps[1][:, 384:384 + CPP]),
            r=["ps1"], w=["hcols"])
        p.add("scalar", lambda e: e.activation(
            hcols[:, 11, :], hcols[:, 10, :], AF.Exp),
            r=["hcols"], w=["hcols"])
        p.add("vector", lambda e: e.tensor_sub(
            hcols[:, 9, :], hcols[:, 10, :], hcols[:, 5, :]),
            r=["hcols"], w=["hcols"])
        p.add("scalar", lambda e: e.activation(
            hcols[:, 9, :], hcols[:, 9, :], AF.Exp),
            r=["hcols"], w=["hcols"])
        for ck in range(CPP):
            emit_instance(ps_i, hh, ck)
        p.add("sync", lambda e: e.dma_start_transpose(ofm[:], ogtm[:].rearrange("p a b -> p (a b)")),
              r=["ogtm"], w=["ofm"], dma=True)
        p.add("sync", lambda e, hh=hh, t0=t0: e.dma_start(
            out=onrm_d[hh, :, t0:t0 + PASS_T],
            in_=ofm[:].rearrange("p a b -> p (a b)")),
            r=["ofm"], dma=True)

    def emit_pass(ps_i):
        t0 = ps_i * PASS_T
        for st in range(8):
            xb = xld[st % 2]
            key = f"xld{st % 2}"
            rows = slice(t0 + st * 128, t0 + st * 128 + 128)
            p.add("sync", lambda e, rows=rows, xb=xb: e.dma_start(
                out=xb[:], in_=x_d[rows, :]), w=[key], dma=True)
            for j in range(4):
                p.add("vector", lambda e, xb=xb, j=j: e.bn_stats(
                    bst[:, j, :], xb[:, j * 512:(j + 1) * 512]),
                    r=[key], w=["bst"], special=True)
            p.add("vector", lambda e: e.bn_aggr(
                lnc_[:, 0:2], bst[:].rearrange("p a b -> p (a b)")),
                r=["bst"], w=["lnc"], special=True)
            p.add("scalar", lambda e: e.activation(
                lnc_[:, 1:2], lnc_[:, 1:2], AF.Ln, bias=cepsc[:, 0:1]),
                r=["lnc", "c_epsc"], w=["lnc"])
            p.add("scalar", lambda e: e.activation(
                lnc_[:, 1:2], lnc_[:, 1:2], AF.Exp, scale=-0.5),
                r=["lnc"], w=["lnc"])
            p.add("vector", lambda e, xb=xb: e.tensor_scalar(
                xtm[:], xb[:], lnc_[:, 0:1], lnc_[:, 1:2],
                ALU.subtract, ALU.mult),
                r=[key, "lnc"], w=["xtm"])
            p.add("sync", lambda e, st=st: e.dma_start_transpose(
                xn[:, :, st * 128:(st + 1) * 128], xtm[:]),
                r=["xtm"], w=["xn"], dma=True)
        if "xn" in dbg_d and ps_i == 0:
            dbg_store("xn", xn[:].rearrange("p a b -> p (a b)"), ["xn"])

        p.add("gpsimd", lambda e: e.dma_start(
            out=wbuf[0][:, :, 0:16],
            in_=wab_d.rearrange("(k q) c -> q k c", q=128)),
            w=["wbuf0"], dma=True)
        for n_ in range(2):
            for k in range(KCH):
                p.add("tensor", lambda e, n_=n_, k=k: e.matmul(
                    ps[0][0:16, 0:512], wbuf[0][:, k, 0:16],
                    xn[:, k, n_ * 512:(n_ + 1) * 512],
                    start=(k == 0), stop=(k == KCH - 1)),
                    r=["wbuf0", "xn"], w=["ps0"])
            p.add("vector", lambda e, n_=n_: e.tensor_copy(
                scf[:, 0, n_ * 512:(n_ + 1) * 512], ps[0][0:16, 0:512]),
                r=["ps0"], w=["scf"])
        p.add("vector", lambda e: e.tensor_scalar(
            scf[:, 1, :], scf[:, 0, :], csgn[:, 0:1], cbia[:, 0:1],
            ALU.mult, ALU.add),
            r=["scf", "c_sgn", "c_bia"], w=["scf"])
        p.add("scalar", lambda e: e.activation(scf[:, 0, :], scf[:, 1, :], AF.Abs),
              r=["scf"], w=["scf"])
        p.add("scalar", lambda e: e.activation(scf[:, 0, :], scf[:, 0, :],
                                               AF.Exp, scale=-1.0),
              r=["scf"], w=["scf"])
        p.add("vector", lambda e: e.tensor_scalar_add(scf[:, 0, :], scf[:, 0, :],
                                                      1.0),
              r=["scf"], w=["scf"])
        p.add("scalar", lambda e: e.activation(scf[:, 0, :], scf[:, 0, :], AF.Ln),
              r=["scf"], w=["scf"])
        p.add("vector", lambda e: e.tensor_relu(scf[:, 1, :], scf[:, 1, :]),
              r=["scf"], w=["scf"])
        p.add("vector", lambda e: e.tensor_add(scf[:, 0, :], scf[:, 0, :],
                                               scf[:, 1, :]),
              r=["scf"], w=["scf"])
        p.add("vector", lambda e: e.tensor_scalar(
            scG[:], scf[:, 0, :], negal[:, 0:1], None, ALU.mult),
            r=["scf", "negal"], w=["scG"])
        for ck in range(CPP):
            p.add("vector", lambda e, ck=ck: e.tensor_tensor_scan(
                scB[:, ck * C:(ck + 1) * C], cones[0:16, 0:128],
                scG[:, ck * C:(ck + 1) * C], 0.0, ALU.mult, ALU.add),
                r=["scG", "c_ones"], w=["scB"], special=True)
        if "sc" in dbg_d and ps_i == 0:
            dbg_store("sc", scG[:], ["scG"])
        if "scB" in dbg_d and ps_i == 0:
            dbg_store("scB", scB[:], ["scB"])
        for ck in range(CPP):
            p.add("tensor", lambda e, ck=ck: e.transpose(
                ps[1][:, 0:16], scG[:, ck * C:(ck + 1) * C], cidf[0:16, 0:16]),
                r=["scG", "c_idf"], w=["ps1"])
            p.add("vector", lambda e, ck=ck: e.tensor_copy(
                scpack[:, ck, 0:16], ps[1][:, 0:16]),
                r=["ps1"], w=["scpack"])
            p.add("tensor", lambda e, ck=ck: e.transpose(
                ps[1][:, 16:32], scB[:, ck * C:(ck + 1) * C], cidf[0:16, 0:16]),
                r=["scB", "c_idf"], w=["ps1"])
            p.add("vector", lambda e, ck=ck: e.tensor_copy(
                scpack[:, ck, 16:32], ps[1][:, 16:32]),
                r=["ps1"], w=["scpack"])
        for hh in range(HG):
            emit_head(ps_i, hh)

    for ps_i in range(NPASS):
        emit_pass(ps_i)

    with nc.Block() as block:
        p.finalize(block)
    return nc, sorted(dbg_d)




def build_tail(dbg=()):
    nc = bass.Bass("TRN2", target_bir_lowering=False, debug=False,
                   detect_race_conditions=False)
    p = Prog(nc)
    TH = 1024

    o_d = nc.dram_tensor("o", [H, 128, TH], BF16, kind="ExternalInput").ap()
    xh_d = nc.dram_tensor("xh", [TH, D], F32, kind="ExternalInput").ap()
    wout_d = nc.dram_tensor("wout", [D, D], F32, kind="ExternalInput").ap()
    wgu_d = nc.dram_tensor("wgu", [D, 2 * INTER], F32, kind="ExternalInput").ap()
    wdn_d = nc.dram_tensor("wdn", [INTER, D], F32, kind="ExternalInput").ap()
    gus_d = nc.dram_tensor("gus", [128, 2 * MI], F32, kind="ExternalInput").ap()
    dns_d = nc.dram_tensor("dns", [128, KCH], F32, kind="ExternalInput").ap()
    epsc_d = nc.dram_tensor("epsc", [128, 4], F32, kind="ExternalInput").ap()
    out_d = nc.dram_tensor("out", [TH, D], F32, kind="ExternalOutput").ap()
    x1s_d = nc.dram_tensor("x1spill", [TH, D], F32, kind="Internal").ap()

    dbg_d = {}
    for nm, shp, dt in [("attn", [128, KCH * TH], BF16),
                        ("x1", [128, D], F32),
                        ("hfm", [128, TH], BF16)]:
        if nm in dbg:
            dbg_d[nm] = nc.dram_tensor("dbg_" + nm, shp, dt,
                                       kind="ExternalOutput").ap()

    Al = nc.alloc_sbuf_tensor
    ofm = Al("ofm", [128, H, TH], BF16)
    wbuf = Al("wbuf", [128, KCH, 512], BF16)
    afm = Al("afm", [128, KCH, TH], BF16)
    atm = Al("atm", [128, D], BF16)
    xld = Al("xld", [128, D], F32)
    x1t = Al("x1t", [128, D], F32)
    x1n = Al("x1n", [128, D], BF16)
    xn1 = Al("xn1", [128, KCH, TH], BF16)
    hfm = Al("hfm", [128, MI, 512], BF16)
    lnc_ = Al("lncols", [128, 4], F32)
    bst = Al("bnst", [128, 4, 6], F32)
    sqs = Al("sqs", [128, 512], F32)
    gus = Al("c_gus", [128, 2 * MI], F32)
    dns = Al("c_dns", [128, KCH], F32)
    cepsc = Al("c_epsc", [128, 4], F32)
    ps = [nc.alloc_psum_tensor(f"ps{i}", [128, 512], F32) for i in range(8)]

    def ln_stats(srckey, src_ap):
        for j in range(4):
            p.add("vector", lambda e, j=j: e.bn_stats(
                bst[:, j, :], src_ap[:, j * 512:(j + 1) * 512]),
                r=[srckey], w=["bst"], special=True)
        p.add("vector", lambda e: e.bn_aggr(
            lnc_[:, 0:2], bst[:].rearrange("p a b -> p (a b)")),
            r=["bst"], w=["lnc"], special=True)
        p.add("scalar", lambda e: e.activation(
            lnc_[:, 1:2], lnc_[:, 1:2], AF.Ln, bias=cepsc[:, 0:1]),
            r=["lnc", "c_epsc"], w=["lnc"])
        p.add("scalar", lambda e: e.activation(
            lnc_[:, 1:2], lnc_[:, 1:2], AF.Exp, scale=-0.5), r=["lnc"], w=["lnc"])

    def ln_residual(srckey, make_xn1):
        for st in range(8):
            rows = slice(st * 128, st * 128 + 128)
            res_dram = xh_d if make_xn1 else x1s_d
            p.add("sync", lambda e, rows=rows, res_dram=res_dram: e.dma_start(
                out=xld[:], in_=res_dram[rows, :]), w=["xld"], dma=True)
            for kc in range(KCH):
                p.add("sync", lambda e, st=st, kc=kc: e.dma_start_transpose(
                    atm[:, kc * 128:(kc + 1) * 128],
                    afm[:, kc, st * 128:(st + 1) * 128]),
                    r=[srckey], w=["atm"], dma=True)
            ln_stats("atm", atm[:])
            p.add("vector", lambda e: e.tensor_scalar(
                x1t[:], atm[:], lnc_[:, 0:1], lnc_[:, 1:2],
                ALU.subtract, ALU.mult),
                r=["atm", "lnc"], w=["x1t"])
            p.add("vector", lambda e: e.tensor_add(x1t[:], x1t[:], xld[:]),
                  r=["x1t", "xld"], w=["x1t"])
            if make_xn1:
                p.add("sync", lambda e, rows=rows: e.dma_start(
                    out=x1s_d[rows, :], in_=x1t[:]), r=["x1t"], dma=True)
                if "x1" in dbg_d and st == 0:
                    p.add("sync", lambda e: e.dma_start(
                        out=dbg_d["x1"], in_=x1t[:]), r=["x1t"], dma=True)
                ln_stats("x1t", x1t[:])
                p.add("vector", lambda e: e.tensor_scalar(
                    x1n[:], x1t[:], lnc_[:, 0:1], lnc_[:, 1:2],
                    ALU.subtract, ALU.mult),
                    r=["x1t", "lnc"], w=["x1n"])
                p.add("sync", lambda e, st=st: e.dma_start_transpose(
                    xn1[:, :, st * 128:(st + 1) * 128], x1n[:]),
                    r=["x1n"], w=["xn1"], dma=True)
            else:
                p.add("sync", lambda e, rows=rows: e.dma_start(
                    out=out_d[rows, :], in_=x1t[:]), r=["x1t"], dma=True)

    p.add("sync", lambda e: e.dma_start(
        out=ofm[:], in_=o_d.rearrange("h q t -> q h t")), w=["ofm"], dma=True)
    p.add("sync", lambda e: e.dma_start(out=gus[:], in_=gus_d), w=["gus"], dma=True)
    p.add("sync", lambda e: e.dma_start(out=dns[:], in_=dns_d), w=["dns"], dma=True)
    p.add("sync", lambda e: e.dma_start(out=cepsc[:], in_=epsc_d), w=["c_epsc"], dma=True)
    p.add("vector", lambda e: e.memset(bst[:], 0.0), w=["bst"])
    p.add("vector", lambda e: e.bn_stats(bst[:, 0, :], bst[:, 2, 0:6]),
          r=["bst"], w=["bst", "warm"])
    p.add("vector", lambda e: e.bn_aggr(bst[:, 1, 0:2], bst[:, 0, :]),
          r=["bst"], w=["bst", "warm"])

    for mg in range(4):
        p.add("gpsimd", lambda e, mg=mg: e.dma_start(
            out=wbuf[:],
            in_=wout_d.rearrange("(k q) c -> q k c", q=128)[:, :, mg * 512:(mg + 1) * 512]),
            w=["wbuf"], dma=True)
        for mt in range(4):
            m = mg * 4 + mt
            for n_ in range(2):
                bank = n_
                for k in range(KCH):
                    p.add("tensor", lambda e, mt=mt, n_=n_, k=k, bank=bank: e.matmul(
                        ps[bank][:, 0:512], wbuf[:, k, mt * 128:(mt + 1) * 128],
                        ofm[:, k, n_ * 512:(n_ + 1) * 512],
                        start=(k == 0), stop=(k == KCH - 1)),
                        r=["wbuf", "ofm"], w=[f"ps{bank}"])
                p.add("vector", lambda e, m=m, n_=n_, bank=bank: e.tensor_copy(
                    afm[:, m, n_ * 512:(n_ + 1) * 512], ps[bank][:, 0:512]),
                    r=[f"ps{bank}"], w=["afm"])
    if "attn" in dbg_d:
        p.add("sync", lambda e: e.dma_start(
            out=dbg_d["attn"], in_=afm[:].rearrange("q k t -> q (k t)")),
            r=["afm"], dma=True)

    ln_residual("afm", True)

    for th2 in range(2):
        tc = slice(th2 * 512, th2 * 512 + 512)
        for mg in range(22):
            p.add("gpsimd", lambda e, mg=mg: e.dma_start(
                out=wbuf[:, :, 0:256],
                in_=wgu_d.rearrange("(k q) c -> q k c", q=128)[:, :, mg * 256:(mg + 1) * 256]),
                w=["wbuf"], dma=True)
            p.add("gpsimd", lambda e, mg=mg: e.dma_start(
                out=wbuf[:, :, 256:512],
                in_=wgu_d.rearrange("(k q) c -> q k c", q=128)[:, :, INTER + mg * 256: INTER + (mg + 1) * 256]),
                w=["wbuf"], dma=True)
            for mt in range(2):
                m = mg * 2 + mt
                for k in range(KCH):
                    p.add("tensor", lambda e, mt=mt, k=k, tc=tc: e.matmul(
                        ps[0][:, 0:512], wbuf[:, k, mt * 128:(mt + 1) * 128],
                        xn1[:, k, tc], start=(k == 0), stop=(k == KCH - 1)),
                        r=["wbuf", "xn1"], w=["ps0"])
                for k in range(KCH):
                    p.add("tensor", lambda e, mt=mt, k=k, tc=tc: e.matmul(
                        ps[1][:, 0:512], wbuf[:, k, 256 + mt * 128: 256 + (mt + 1) * 128],
                        xn1[:, k, tc], start=(k == 0), stop=(k == KCH - 1)),
                        r=["wbuf", "xn1"], w=["ps1"])
                p.add("scalar", lambda e, m=m: e.activation(
                    sqs[:, 0:512], ps[0][:, 0:512], AF.Silu,
                    scale=gus[:, m:m + 1]),
                    r=["ps0", "gus"], w=["sqs"])
                p.add("vector", lambda e, m=m: e.scalar_tensor_tensor(
                    hfm[:, m, :], ps[1][:, 0:512], gus[:, MI + m: MI + m + 1],
                    sqs[:, 0:512], ALU.mult, ALU.mult),
                    r=["ps1", "gus", "sqs"], w=["hfm"])
        if "hfm" in dbg_d and th2 == 0:
            p.add("sync", lambda e: e.dma_start(
                out=dbg_d["hfm"][:, 0:512], in_=hfm[:, 0, :]), r=["hfm"], dma=True)
        for mg in range(4):
            for kg in range(4):
                p.add("gpsimd", lambda e, mg=mg, kg=kg: e.dma_start(
                    out=wbuf[:, 0:11, :],
                    in_=wdn_d.rearrange("(k q) c -> q k c", q=128)[:, kg * 11:(kg + 1) * 11, mg * 512:(mg + 1) * 512]),
                    w=["wbuf"], dma=True)
                for mt in range(4):
                    bank = 4 + mt
                    for kk in range(11):
                        kglob = kg * 11 + kk
                        p.add("tensor", lambda e, mt=mt, kk=kk, kglob=kglob, bank=bank, kg=kg: e.matmul(
                            ps[bank][:, 0:512], wbuf[:, kk, mt * 128:(mt + 1) * 128],
                            hfm[:, kglob, :],
                            start=(kg == 0 and kk == 0),
                            stop=(kg == 3 and kk == 10)),
                            r=["wbuf", "hfm"], w=[f"ps{bank}"])
            for mt in range(4):
                m = mg * 4 + mt
                bank = 4 + mt
                p.add("vector", lambda e, m=m, bank=bank, tc=tc: e.tensor_scalar(
                    afm[:, m, tc], ps[bank][:, 0:512], dns[:, m:m + 1], None,
                    ALU.mult),
                    r=[f"ps{bank}", "dns"], w=["afm"])

    ln_residual("afm", False)

    with nc.Block() as block:
        p.finalize(block)
    return nc, sorted(dbg_d)


def _consts():
    identf = np.eye(128, dtype=np.float32)
    ibf = np.eye(128, dtype=BF)
    onesf = np.ones((128, 128), np.float32)
    e127 = np.zeros((128, 128), np.float32)
    e127[127, :] = 1.0
    ii = np.arange(128)
    m_strictlow = (ii[:, None] > ii[None, :]).astype(np.float32)
    m_strictup = (ii[:, None] < ii[None, :]).astype(np.float32)
    m_upincl = (ii[:, None] <= ii[None, :]).astype(np.float32)
    maskm = np.ascontiguousarray(
        np.concatenate([m_strictlow, m_strictup, m_upincl], axis=1))
    maska = np.ascontiguousarray((1.0 - maskm) * FILL)
    epsc = np.zeros((128, 4), np.float32)
    epsc[:, 0] = LNEPS
    epsc[:, 1] = EPS6
    epsc[:, 2] = 1.0
    return identf, ibf, onesf, e127, maskm, maska, epsc


def kernel(**inputs):
    xfull = np.ascontiguousarray(np.asarray(inputs["L0_attn_hidden_states"],
                                            np.float32))
    Wqkv = np.asarray(inputs["L0_attn_W_qkv"], np.float32)
    convw = np.asarray(inputs["L0_attn_conv_w"], np.float32)
    Wz = np.asarray(inputs["L0_attn_W_z"], np.float32)
    Wb = np.asarray(inputs["L0_attn_W_b"], np.float32)
    Wa = np.asarray(inputs["L0_attn_W_a"], np.float32)
    dtb = np.asarray(inputs["L0_attn_dt_bias"], np.float32)
    Alog = np.asarray(inputs["L0_attn_A_log"], np.float32)
    nw = np.asarray(inputs["L0_attn_norm_weight"], np.float32)
    Wout = np.ascontiguousarray(np.asarray(inputs["L0_attn_W_out"], np.float32))
    gw = np.asarray(inputs["L0_ffn_gate_w"], np.float32)
    uw = np.asarray(inputs["L0_ffn_up_w"], np.float32)
    dw = np.asarray(inputs["L0_ffn_down_w"], np.float32)
    gs = np.asarray(inputs["L0_ffn_gate_s"], np.float32)
    us = np.asarray(inputs["L0_ffn_up_s"], np.float32)
    ds = np.asarray(inputs["L0_ffn_down_s"], np.float32)

    identf, ibf, onesf, e127, maskm, maska, epsc = _consts()
    nwt = np.ascontiguousarray(np.broadcast_to(nw[None, :], (128, 128)).astype(np.float32))

    in_maps1 = []
    for c in range(8):
        b, hg = c // 2, c % 2
        whead = np.empty((D, HG * 512), np.float32)
        conv_flat = np.empty((HG * 3 * 128, 4), np.float32)
        for h in range(HG):
            gh = hg * HG + h
            whead[:, h * 512 + 0:h * 512 + 128] = Wqkv[:, gh * 128:(gh + 1) * 128]
            whead[:, h * 512 + 128:h * 512 + 256] = Wqkv[:, 2048 + gh * 128: 2048 + (gh + 1) * 128]
            whead[:, h * 512 + 256:h * 512 + 384] = Wqkv[:, 4096 + gh * 128: 4096 + (gh + 1) * 128]
            whead[:, h * 512 + 384:h * 512 + 512] = Wz[:, gh * 128:(gh + 1) * 128]
            for m, base in enumerate((0, 2048, 4096)):
                conv_flat[(h * 3 + m) * 128:(h * 3 + m + 1) * 128, :] = \
                    convw[base + gh * 128: base + (gh + 1) * 128, 0, :]
        wab = np.ascontiguousarray(np.concatenate(
            [Wb[:, hg * 8:hg * 8 + 8], Wa[:, hg * 8:hg * 8 + 8]], axis=1))
        spsign = np.array([[-1.0]] * 8 + [[1.0]] * 8, np.float32)
        spbias = np.concatenate([np.zeros((8, 1), np.float32),
                                 dtb[hg * 8:hg * 8 + 8, None]], axis=0)
        alog8 = np.concatenate([np.zeros((8, 1), np.float32),
                                Alog[hg * 8:hg * 8 + 8, None]], axis=0)
        in_maps1.append(dict(
            x=xfull[b], whead=np.ascontiguousarray(whead),
            convw=conv_flat, wab=wab, spsign=spsign,
            spbias=np.ascontiguousarray(spbias), alog=alog8, nw=nwt,
            identf=identf, ibf=ibf, onesf=onesf, e127=e127,
            maskm=maskm, maska=maska, epsc=epsc))

    nc1, _ = build_attn()
    res1 = run_bass_kernel_spmd(nc1, in_maps1, core_ids=list(range(8)))
    onrm = [np.asarray(res1.results[c]["onrm"]) for c in range(8)]

    wgu = np.ascontiguousarray(
        np.concatenate([gw.T, uw.T], axis=1).astype(np.float32))
    wdnT = np.ascontiguousarray(dw.T)
    gus = np.ascontiguousarray(np.concatenate(
        [gs.reshape(MI, 128).T, us.reshape(MI, 128).T], axis=1))
    dns = np.ascontiguousarray(ds.reshape(KCH, 128).T)

    in_maps2 = []
    for c in range(8):
        b, th = c // 2, c % 2
        sl = slice(th * 1024, th * 1024 + 1024)
        o_all = np.ascontiguousarray(np.concatenate(
            [onrm[2 * b][:, :, sl], onrm[2 * b + 1][:, :, sl]], axis=0))
        in_maps2.append(dict(
            o=o_all, xh=np.ascontiguousarray(xfull[b, sl]), wout=Wout,
            wgu=wgu, wdn=wdnT, gus=gus, dns=dns, epsc=epsc))

    nc2, _ = build_tail()
    res2 = run_bass_kernel_spmd(nc2, in_maps2, core_ids=list(range(8)))

    out = np.empty((B, T, D), np.float32)
    for c in range(8):
        b, th = c // 2, c % 2
        out[b, th * 1024:(th + 1) * 1024] = np.asarray(res2.results[c]["out"],
                                                       np.float32)
    return out


# version 04
# speedup vs baseline: 1.1561x; 1.1561x over previous
import math
import numpy as np
import ml_dtypes
import concourse.bass as bass
import concourse.mybir as mybir
from concourse.bass_utils import run_bass_kernel_spmd

F32 = mybir.dt.float32
BF16 = mybir.dt.bfloat16
AF = mybir.ActivationFunctionType
ALU = mybir.AluOpType
AX = mybir.AxisListType
BF = ml_dtypes.bfloat16

B, T, D = 4, 2048, 2048
H, DK, DV = 16, 128, 128
HG = 8
KCH = 16
C = 128
PASS_T = 1024
NPASS = 2
CPP = PASS_T // C
INTER = 5632
MI = INTER // 128
FILL = -30000.0
LNEPS = 1e-5
EPS6 = 1e-6
LNDK = -0.5 * math.log(DK)


class Prog:

    ENGS = ("sync", "gpsimd", "vector", "scalar", "tensor")

    def __init__(self, nc):
        self.nc = nc
        self.ops = []

    def add(self, eng, emit, r=(), w=(), dma=False, special=False):
        if special:
            r = tuple(r) + ("warm",)
        self.ops.append((eng, emit, tuple(r), tuple(w), bool(dma), bool(special)))

    def finalize(self, block):
        n = len(self.ops)
        deps = [set() for _ in range(n)]
        last_w = {}
        readers = {}
        for i, (eng, emit, reads, writes, dma, special) in enumerate(self.ops):
            for b in reads:
                if b in last_w:
                    deps[i].add(last_w[b])
            for b in writes:
                if b in last_w:
                    deps[i].add(last_w[b])
                for rd in readers.get(b, ()):
                    deps[i].add(rd)
            for b in reads:
                readers.setdefault(b, []).append(i)
            for b in writes:
                last_w[b] = i
                readers[b] = []
        UNSAFE = ("vector", "gpsimd")
        need = [False] * n
        for i in range(n):
            eng, _, _, _, dma, special = self.ops[i]
            if dma:
                need[i] = True
            for d in deps[i]:
                if (self.ops[d][4] or self.ops[d][0] != eng or special
                        or self.ops[d][5] or eng in UNSAFE):
                    need[d] = True
        sems = {}
        for e in self.ENGS:
            sems[(e, False)] = self.nc.alloc_semaphore(f"sb_{e}_c")
            sems[(e, True)] = self.nc.alloc_semaphore(f"sb_{e}_d")
        counter = {k: 0 for k in sems}
        val = [None] * n
        for i in range(n):
            if need[i]:
                k = (self.ops[i][0], self.ops[i][4])
                counter[k] += 16 if self.ops[i][4] else 1
                val[i] = (k, counter[k])
        by_eng = {e: [] for e in self.ENGS}
        for i, op in enumerate(self.ops):
            by_eng[op[0]].append(i)

        def mk_section(eng_name, idxs):
            def section(eng):
                wm = {}
                for i in idxs:
                    _, emit, _, _, dma, special = self.ops[i]
                    waits = {}
                    for d in deps[i]:
                        d_eng, d_dma, d_sp = (self.ops[d][0], self.ops[d][4],
                                              self.ops[d][5])
                        if (not d_dma and d_eng == eng_name
                                and not (special or d_sp)
                                and eng_name not in ("vector", "gpsimd")):
                            continue
                        k, v = val[d]
                        if waits.get(k, 0) < v:
                            waits[k] = v
                    for k in sorted(waits, key=str):
                        v = waits[k]
                        if wm.get(k, 0) < v:
                            eng.wait_ge(sems[k], v)
                            wm[k] = v
                    inst = emit(eng)
                    if need[i]:
                        k, v = val[i]
                        inst.then_inc(sems[k], 16 if k[1] else 1)
            return section

        for e in self.ENGS:
            if by_eng[e]:
                getattr(block, e)(mk_section(e, by_eng[e]))
        finals = [(k, v) for k, v in counter.items() if v > 0]

        def tail(eng):
            for k, v in sorted(finals, key=lambda kv: str(kv[0])):
                eng.wait_ge(sems[k], v)
        block.sync(tail)


RG = [[0, 1], [2, 3], [4, 5], [6, 7]]


def build_fused(dbg=()):
    nc = bass.Bass("TRN2", target_bir_lowering=False, debug=False,
                   detect_race_conditions=False)
    p = Prog(nc)
    TH = 1024

    x_d = nc.dram_tensor("x", [T, D], F32, kind="ExternalInput").ap()
    whead_d = nc.dram_tensor("whead", [D, HG * 512], F32, kind="ExternalInput").ap()
    convw_d = nc.dram_tensor("convw", [HG * 3 * 128, 4], F32, kind="ExternalInput").ap()
    wab_d = nc.dram_tensor("wab", [D, 16], F32, kind="ExternalInput").ap()
    spsign_d = nc.dram_tensor("spsign", [16, 1], F32, kind="ExternalInput").ap()
    spbias_d = nc.dram_tensor("spbias", [16, 1], F32, kind="ExternalInput").ap()
    alog_d = nc.dram_tensor("alog", [16, 1], F32, kind="ExternalInput").ap()
    nw_d = nc.dram_tensor("nw", [128, 128], F32, kind="ExternalInput").ap()
    identf_d = nc.dram_tensor("identf", [128, 128], F32, kind="ExternalInput").ap()
    ibf_d = nc.dram_tensor("ibf", [128, 128], BF16, kind="ExternalInput").ap()
    onesf_d = nc.dram_tensor("onesf", [128, 128], F32, kind="ExternalInput").ap()
    e127_d = nc.dram_tensor("e127", [128, 128], F32, kind="ExternalInput").ap()
    epsc_d = nc.dram_tensor("epsc", [128, 4], F32, kind="ExternalInput").ap()
    maskm_d = nc.dram_tensor("maskm", [128, 3 * 128], F32, kind="ExternalInput").ap()
    maska_d = nc.dram_tensor("maska", [128, 3 * 128], F32, kind="ExternalInput").ap()

    xh_d = nc.dram_tensor("xh", [TH, D], F32, kind="ExternalInput").ap()
    wout_d = nc.dram_tensor("wout", [D, D], F32, kind="ExternalInput").ap()
    wgu_d = nc.dram_tensor("wgu", [D, 2 * INTER], F32, kind="ExternalInput").ap()
    wdn_d = nc.dram_tensor("wdn", [INTER, D], F32, kind="ExternalInput").ap()
    gus_d = nc.dram_tensor("gus", [128, 2 * MI], F32, kind="ExternalInput").ap()
    dns_d = nc.dram_tensor("dns", [128, KCH], F32, kind="ExternalInput").ap()
    out_d = nc.dram_tensor("out", [TH, D], F32, kind="ExternalOutput").ap()
    x1s_d = nc.dram_tensor("x1spill", [TH, D], F32, kind="Internal").ap()
    cc_in = nc.dram_tensor("cc_in", [NPASS, HG, 128, TH], BF16,
                           kind="Internal").ap()
    cc_out = nc.dram_tensor("cc_out", [2, NPASS, HG, 128, TH], BF16,
                            kind="Internal").ap()
    dbg_d = {}
    for nm, shp, dt in [
        ("xn", [128, KCH * PASS_T], BF16), ("qkv", [4, 128, PASS_T], BF16),
        ("sc", [16, PASS_T], F32), ("scB", [16, PASS_T], F32), ("ktm", [128, CPP * 128], BF16),
        ("TT", [128, 128], BF16), ("U", [128, 128], BF16),
        ("S", [128, 128], F32), ("AL", [128, 128], BF16),
        ("expin", [128, 3 * 128], F32), ("hcols", [128, 14], F32),
        ("attn", [128, KCH * 1024], BF16), ("x1", [128, D], F32),
        ("hfm", [128, 1024], BF16),
    ]:
        if nm in dbg:
            dbg_d[nm] = nc.dram_tensor("dbg_" + nm, shp, dt,
                                       kind="ExternalOutput").ap()

    Al = nc.alloc_sbuf_tensor
    REGION = 98304
    base0, _ = nc.bump_sbuf(REGION, align_bytes=256)
    _cur = [0]

    def Ar(name, shape, dtype):
        import math as _m
        nbytes = _m.prod(shape[1:]) * (2 if dtype == BF16 else 4)
        off = (_cur[0] + 255) // 256 * 256
        _cur[0] = off + nbytes
        assert _cur[0] <= REGION, (name, _cur[0])
        return nc.alloc_sbuf_tensor_at(name, shape, dtype, offset=base0 + off)

    def ArAt(name, shape, dtype, off):
        return nc.alloc_sbuf_tensor_at(name, shape, dtype, offset=base0 + off)

    xn = Al("xn", [128, KCH, PASS_T], BF16)
    xld = [Al(f"xld{i}", [128, D], F32) for i in range(2)]
    xtm = Ar("xtm", [128, D], BF16)
    lnc_ = Al("lncols", [128, 4], F32)
    bst = Al("bnst", [128, 4, 6], F32)
    wbuf = [Al(f"wbuf{i}", [128, KCH, 512], BF16) for i in range(2)]
    convw = Al("convwt", [128, HG * 3, 4], F32)
    mixs = [Ar(f"mixs{i}", [128, 1027], F32) for i in range(2)]
    cvac = Ar("cvac", [128, 1024], F32)
    tails = Al("tails", [128, HG * 3, 3], F32)
    qf = Ar("qf", [128, PASS_T], BF16)
    kf = Ar("kf", [128, PASS_T], BF16)
    vf = Ar("vf", [128, PASS_T], BF16)
    zst = Ar("zst", [128, PASS_T], BF16)
    qtm = Ar("qtm", [128, CPP, 128], BF16)
    ktm = Ar("ktm", [128, CPP, 128], BF16)
    vtm = Ar("vtm", [128, CPP, 128], BF16)
    ztm = Ar("ztm", [128, HG, CPP, 128], BF16)
    ogtm = Ar("ogtm", [128, CPP, 128], BF16)
    ofm = Ar("ofm", [128, CPP, 128], BF16)
    scG = Al("scG", [16, PASS_T], F32)
    scf = Ar("scf", [16, 2, PASS_T], F32)
    scpack = Al("scpack", [128, CPP, 32], F32)
    hcols = Al("hcols", [128, 14, CPP], F32)
    rcol = Al("rcol", [128, 2], F32)
    colbc = Ar("colbc", [128, 128], F32)
    expin = Ar("expin", [128, 3, 128], F32)
    expout = Ar("expout", [128, 3, 128], BF16)
    AL_ = Ar("tAL", [128, 128], BF16)
    AU_ = Ar("tAU", [128, 128], BF16)
    AUn_ = Ar("tAUn", [128, 128], BF16)
    MT_ = Ar("tMT", [128, 128], BF16)
    pw = {}
    for lv in (2, 4, 8, 16, 32):
        pw[(lv, "L")] = Ar(f"A{lv}L", [128, 128], BF16)
        pw[(lv, "U")] = Ar(f"A{lv}U", [128, 128], BF16)
    pw[(64, "L")] = Ar("A64L", [128, 128], BF16)
    X_ = Ar("Xtt", [128, 128], BF16)
    Rv = Ar("Rv", [128, 128], BF16)
    Rk = Ar("Rk", [128, 128], BF16)
    Kt_ = Ar("Kt", [128, 128], BF16)
    Wfm = Ar("Wfm", [128, 128], BF16)
    Utm = Ar("Utm", [128, 128], BF16)
    S_f = Ar("S_f", [128, HG, 128], F32)
    S_b = Ar("S_b", [128, 128], BF16)
    gts = Ar("gts", [128, 128], F32)
    sq2 = Ar("sq2", [128, CPP, 128], F32)
    cnw = Al("c_nw", [128, 128], F32)
    cidf = Al("c_identf", [128, 128], F32)
    cibf = Al("c_ibf", [128, 128], BF16)
    cones = Al("c_onesf", [128, 128], F32)
    ce127 = Al("c_e127", [128, 128], F32)
    cepsc = Al("c_epsc", [128, 4], F32)
    cmaskm = Al("c_maskm", [128, 3, 128], F32)
    cmaska = Al("c_maska", [128, 3, 128], F32)
    csgn = Al("c_spsign", [16, 1], F32)
    cbia = Al("c_spbias", [16, 1], F32)
    calog = Al("c_alog", [16, 1], F32)
    negal = Al("negal", [16, 1], F32)
    scB = Ar("scB", [16, PASS_T], F32)
    afm = ArAt("afm", [128, KCH, TH], BF16, 0)
    hfm = ArAt("hfm", [128, MI, 512], BF16, 0)
    ofm_t = ArAt("ofm_t", [128, H, TH], BF16, 45056)
    atm = ArAt("atm", [128, D], BF16, 77824)
    x1t = ArAt("x1t", [128, D], F32, 81920)
    x1n = ArAt("x1n", [128, D], BF16, 90112)
    sqs = ArAt("sqs", [128, 512], F32, 94208)
    gus = Al("c_gus", [128, 2 * MI], F32)
    dns = Al("c_dns", [128, KCH], F32)
    ps = [nc.alloc_psum_tensor(f"ps{i}", [128, 512], F32) for i in range(8)]

    def dbg_store(nm, src, rkeys):
        if nm in dbg_d:
            p.add("sync", lambda e, nm=nm, src=src: e.dma_start(
                out=dbg_d[nm], in_=src), r=rkeys, dma=True)

    for dst, key, dr in [(cnw, "c_nw", nw_d), (cidf, "c_idf", identf_d),
                         (cibf, "c_ibf", ibf_d), (cones, "c_ones", onesf_d),
                         (ce127, "c_e127", e127_d), (csgn, "c_sgn", spsign_d),
                         (cbia, "c_bia", spbias_d),
                         (cepsc, "c_epsc", epsc_d),
                         (cmaskm, "c_maskm", maskm_d),
                         (cmaska, "c_maska", maska_d)]:
        p.add("sync", lambda e, dst=dst, dr=dr: e.dma_start(
            out=dst[:] if len(dst.shape) == 2 else dst[:].rearrange("p a b -> p (a b)"),
            in_=dr), w=[key], dma=True)
    p.add("sync", lambda e: e.dma_start(out=calog[:], in_=alog_d),
          w=["c_alog"], dma=True)
    p.add("sync", lambda e: e.dma_start(
        out=convw[:], in_=convw_d.rearrange("(g q) k -> q g k", q=128)),
        w=["convw"], dma=True)
    p.add("scalar", lambda e: e.activation(negal[:], calog[:], AF.Exp),
          r=["c_alog"], w=["negal"])
    p.add("vector", lambda e: e.tensor_scalar(negal[:], negal[:], csgn[:, 0:1],
                                              -1.0, ALU.mult, ALU.mult),
          r=["negal", "c_sgn"], w=["negal"])
    p.add("vector", lambda e: e.memset(tails[:], 0.0), w=["tails"])
    p.add("vector", lambda e: e.memset(S_f[:], 0.0), w=["S_f"])
    p.add("vector", lambda e: e.memset(bst[:], 0.0), w=["bst"])
    p.add("vector", lambda e: e.bn_stats(bst[:, 0, :], bst[:, 2, 0:6]),
          r=["bst"], w=["bst", "warm"])
    p.add("vector", lambda e: e.bn_aggr(bst[:, 1, 0:2], bst[:, 0, :]),
          r=["bst"], w=["bst", "warm"])
    p.add("vector", lambda e: e.tensor_tensor_scan(
        bst[:, 2, 0:6], bst[:, 3, 0:6], bst[:, 2, 0:6], 0.0,
        ALU.mult, ALU.add),
        r=["bst"], w=["bst", "warm"])

    def emit_instance(ps_i, hh, ck):
        first = (ps_i == 0 and ck == 0)
        dbg_on = ("TT" in dbg_d and ps_i == 0 and hh == 0 and ck == 1)
        for v, (growcol, subcol, neg) in enumerate(
                [(5, 4, True), (4, 5, False), (6, 5, False)]):
            p.add("vector", lambda e, gc=growcol: e.tensor_scalar(
                colbc[:], cones[:], hcols[:, gc, ck:ck + 1], None, ALU.mult),
                r=["c_ones", "hcols"], w=["colbc"])
            p.add("tensor", lambda e, v=v: e.transpose(
                ps[1][:, v * 128:(v + 1) * 128], colbc[:], cidf[:]),
                r=["colbc", "c_idf"], w=["ps1"])
            if neg:
                p.add("vector", lambda e, v=v, sc=subcol: e.tensor_scalar(
                    expin[:, v, :], ps[1][:, v * 128:(v + 1) * 128],
                    hcols[:, sc, ck:ck + 1], -1.0, ALU.subtract, ALU.mult),
                    r=["ps1", "hcols"], w=["expin"])
            else:
                p.add("vector", lambda e, v=v, sc=subcol: e.tensor_scalar(
                    expin[:, v, :], ps[1][:, v * 128:(v + 1) * 128],
                    hcols[:, sc, ck:ck + 1], None, ALU.subtract),
                    r=["ps1", "hcols"], w=["expin"])
        p.add("gpsimd", lambda e: e.tensor_mul(
            expin[:].rearrange("p a b -> p (a b)"),
            expin[:].rearrange("p a b -> p (a b)"),
            cmaskm[:].rearrange("p a b -> p (a b)")),
            r=["expin", "c_maskm"], w=["expin"])
        p.add("gpsimd", lambda e: e.tensor_add(
            expin[:].rearrange("p a b -> p (a b)"),
            expin[:].rearrange("p a b -> p (a b)"),
            cmaska[:].rearrange("p a b -> p (a b)")),
            r=["expin", "c_maska"], w=["expin"])
        if "expin" in dbg_d and dbg_on:
            dbg_store("expin", expin[:].rearrange("p a b -> p (a b)"), ["expin"])
            p.add("sync", lambda e: e.dma_start(
                out=dbg_d["hcols"],
                in_=hcols[:, :, ck:ck + 1].rearrange("p a b -> p (a b)"),
                allow_slow_non_contiguous=True),
                r=["hcols"], dma=True) if "hcols" in dbg_d else None
        p.add("scalar", lambda e: e.activation(
            expout[:].rearrange("p a b -> p (a b)"),
            expin[:].rearrange("p a b -> p (a b)"), AF.Exp),
            r=["expin"], w=["expout"])
        p.add("tensor", lambda e: e.matmul(ps[0][:, 0:128], kf[:, ck * 128:(ck + 1) * 128],
                                           kf[:, ck * 128:(ck + 1) * 128],
                                           start=True, stop=True),
              r=["kf"], w=["ps0"])
        p.add("tensor", lambda e: e.matmul(ps[0][:, 128:256], kf[:, ck * 128:(ck + 1) * 128],
                                           qf[:, ck * 128:(ck + 1) * 128],
                                           start=True, stop=True),
              r=["kf", "qf"], w=["ps0"])
        p.add("vector", lambda e: e.tensor_mul(AL_[:], ps[0][:, 0:128],
                                               expout[:, 0, :]),
              r=["ps0", "expout"], w=["AL"])
        p.add("vector", lambda e: e.tensor_mul(AU_[:], ps[0][:, 0:128],
                                               expout[:, 1, :]),
              r=["ps0", "expout"], w=["AU"])
        p.add("vector", lambda e: e.tensor_scalar_mul(AUn_[:], AU_[:], -1.0),
              r=["AU"], w=["AUn"])
        p.add("vector", lambda e: e.tensor_mul(MT_[:], ps[0][:, 128:256],
                                               expout[:, 2, :]),
              r=["ps0", "expout"], w=["MT"])
        if "AL" in dbg_d and dbg_on:
            dbg_store("AL", AL_[:], ["AL"])
        chain = [((2, "L"), AU_, AL_), ((2, "U"), AL_, AU_)]
        for lv in (4, 8, 16, 32):
            chain.append(((lv, "L"), pw[(lv // 2, "U")], pw[(lv // 2, "L")]))
            chain.append(((lv, "U"), pw[(lv // 2, "L")], pw[(lv // 2, "U")]))
        chain.append(((64, "L"), pw[(32, "U")], pw[(32, "L")]))
        for idx, (key, lhs, rhs) in enumerate(chain):
            bank = 4 + idx % 2
            bkey = f"ps{bank}"
            def lname(t):
                if t is AU_:
                    return "AU"
                if t is AL_:
                    return "AL"
                for kk, vv in pw.items():
                    if vv is t:
                        return f"A{kk[0]}{kk[1]}"
                raise KeyError
            p.add("tensor", lambda e, bank=bank, lhs=lhs, rhs=rhs: e.matmul(
                ps[bank][:, 0:128], lhs[:], rhs[:], start=True, stop=True),
                r=[lname(lhs), lname(rhs)], w=[bkey])
            p.add("scalar", lambda e, bank=bank, key=key: e.copy(
                pw[key][:], ps[bank][:, 0:128]),
                r=[bkey], w=[f"A{key[0]}{key[1]}"])
        p.add("vector", lambda e: e.tensor_add(X_[:], cibf[:], AUn_[:]),
              r=["c_ibf", "AUn"], w=["X"])
        for li, lv in enumerate((2, 4, 8, 16, 32, 64)):
            bank = 4 + li % 2
            bkey = f"ps{bank}"
            p.add("tensor", lambda e, bank=bank, lv=lv: e.matmul(
                ps[bank][:, 0:128], pw[(lv, "L")][:], X_[:],
                start=True, stop=False),
                r=[f"A{lv}L", "X"], w=[bkey])
            p.add("tensor", lambda e, bank=bank: e.matmul(
                ps[bank][:, 0:128], cibf[:], X_[:], start=False, stop=True),
                r=["c_ibf", "X"], w=[bkey])
            p.add("scalar", lambda e, bank=bank: e.copy(X_[:], ps[bank][:, 0:128]),
                  r=[bkey], w=["X"])
        if "TT" in dbg_d and dbg_on:
            dbg_store("TT", X_[:], ["X"])
        p.add("vector", lambda e: e.tensor_scalar(
            Rv[:], vtm[:, ck, :], hcols[:, 7, ck:ck + 1], None, ALU.mult),
            r=["vtm", "hcols"], w=["Rv"])
        p.add("vector", lambda e: e.tensor_scalar(
            Rk[:], ktm[:, ck, :], hcols[:, 8, ck:ck + 1], None, ALU.mult),
            r=["ktm", "hcols"], w=["Rk"])
        p.add("vector", lambda e: e.tensor_scalar(
            Kt_[:], ktm[:, ck, :], hcols[:, 9, ck:ck + 1], None, ALU.mult),
            r=["ktm", "hcols"], w=["Kt"])
        p.add("scalar", lambda e: e.copy(S_b[:], S_f[:, hh, :]),
              r=["S_f"], w=["S_b"])
        p.add("tensor", lambda e: e.matmul(ps[6][:, 0:128], Rk[:], X_[:],
                                           start=True, stop=True),
              r=["Rk", "X"], w=["ps6"])
        p.add("scalar", lambda e: e.copy(Wfm[:], ps[6][:, 0:128]),
              r=["ps6"], w=["Wfm"])
        p.add("tensor", lambda e: e.matmul(ps[6][:, 128:256], X_[:], Rv[:],
                                           start=True, stop=False),
              r=["X", "Rv"], w=["ps6"])
        p.add("tensor", lambda e: e.matmul(ps[6][:, 128:256], Wfm[:], S_b[:],
                                           start=False, stop=True),
              r=["Wfm", "S_b"], w=["ps6"])
        p.add("vector", lambda e: e.tensor_copy(Utm[:], ps[6][:, 128:256]),
              r=["ps6"], w=["Utm"])
        if "U" in dbg_d and dbg_on:
            dbg_store("U", Utm[:], ["Utm"])
        p.add("tensor", lambda e: e.matmul(ps[7][:, 0:128],
                                           qf[:, ck * 128:(ck + 1) * 128], S_b[:],
                                           start=True, stop=True),
              r=["qf", "S_b"], w=["ps7"])
        p.add("tensor", lambda e: e.matmul(ps[7][:, 128:256], MT_[:], Utm[:],
                                           start=True, stop=True),
              r=["MT", "Utm"], w=["ps7"])
        p.add("tensor", lambda e: e.matmul(ps[6][:, 256:384], Kt_[:], Utm[:],
                                           start=True, stop=True),
              r=["Kt", "Utm"], w=["ps6"])
        p.add("vector", lambda e: e.scalar_tensor_tensor(
            S_f[:, hh, :], S_f[:, hh, :], hcols[:, 11, ck:ck + 1], ps[6][:, 256:384],
            ALU.mult, ALU.add),
            r=["S_f", "hcols", "ps6"], w=["S_f"])
        if "S" in dbg_d and ps_i == 0 and hh == 0 and ck == 1:
            dbg_store("S", S_f[:, 0, :], ["S_f"])
        p.add("scalar", lambda e: e.activation(gts[:], ps[7][:, 0:128],
                                               AF.Copy, scale=hcols[:, 12, ck:ck + 1]),
              r=["ps7", "hcols"], w=["gts"])
        p.add("vector", lambda e: e.tensor_add(gts[:], gts[:], ps[7][:, 128:256]),
              r=["gts", "ps7"], w=["gts"])
        p.add("vector", lambda e: e.tensor_mul(gts[:], gts[:], ztm[:, hh, ck, :]),
              r=["gts", "ztm"], w=["gts"])
        p.add("vector", lambda e: e.tensor_mul(sq2[:, 0, :], gts[:], gts[:]),
              r=["gts"], w=["sq2"])
        p.add("vector", lambda e: e.reduce_sum(rcol[:, 0:1], sq2[:, 0, :], axis=AX.X),
              r=["sq2"], w=["rcol"])
        p.add("vector", lambda e: e.tensor_scalar(
            rcol[:, 0:1], rcol[:, 0:1], 1.0 / DV, EPS6, ALU.mult, ALU.add),
            r=["rcol"], w=["rcol"])
        p.add("scalar", lambda e: e.activation(rcol[:, 0:1], rcol[:, 0:1],
                                               AF.Ln),
              r=["rcol"], w=["rcol"])
        p.add("scalar", lambda e: e.activation(rcol[:, 0:1], rcol[:, 0:1],
                                               AF.Exp, scale=-0.5),
              r=["rcol"], w=["rcol"])
        p.add("vector", lambda e, ck=ck: e.scalar_tensor_tensor(
            ogtm[:, ck, :], gts[:], rcol[:, 0:1], cnw[:], ALU.mult, ALU.mult),
            r=["gts", "rcol", "c_nw"], w=["ogtm"])
        if "orow" in dbg_d and dbg_on:
            pass

    def emit_head(ps_i, hh):
        t0 = ps_i * PASS_T
        wb = wbuf[hh % 2]
        wkey = f"wbuf{hh % 2}"
        p.add("gpsimd", lambda e: e.dma_start(
            out=wb[:],
            in_=whead_d.rearrange("(k q) c -> q k c", q=128)[:, :, hh * 512:(hh + 1) * 512]),
            w=[wkey], dma=True)
        outb = {0: qf, 1: kf, 2: vf, 3: zst}
        okey = {0: "qf", 1: "kf", 2: "vf", 3: "zst"}
        for m in range(4):
            mx = mixs[m % 2]
            mxkey = f"mixs{m % 2}"
            for n_ in range(2):
                bank = 2 + n_
                bkey = f"ps{bank}"
                for k in range(KCH):
                    p.add("tensor", lambda e, m=m, n_=n_, k=k, bank=bank: e.matmul(
                        ps[bank][:, 0:512], wb[:, k, m * 128:(m + 1) * 128],
                        xn[:, k, n_ * 512:(n_ + 1) * 512],
                        start=(k == 0), stop=(k == KCH - 1)),
                        r=[wkey, "xn"], w=[bkey])
                if m < 3:
                    p.add("vector", lambda e, n_=n_, bank=bank, mx=mx: e.tensor_copy(
                        mx[:, 3 + n_ * 512: 3 + (n_ + 1) * 512], ps[bank][:, 0:512]),
                        r=[bkey], w=[mxkey])
                else:
                    p.add("scalar", lambda e, n_=n_, bank=bank: e.activation(
                        zst[:, n_ * 512:(n_ + 1) * 512], ps[bank][:, 0:512], AF.Silu),
                        r=[bkey], w=["zst"])
            if m == 3:
                p.add("sync", lambda e: e.dma_start_transpose(
                    ztm[:, hh, :, :], zst[:]),
                    r=["zst"], w=["ztm"], dma=True)
                continue
            gi = hh * 3 + m
            p.add("vector", lambda e, mx=mx, gi=gi: e.tensor_copy(
                mx[:, 0:3], tails[:, gi, :]),
                r=["tails"], w=[mxkey])
            p.add("vector", lambda e, mx=mx, gi=gi: e.tensor_scalar(
                cvac[:], mx[:, 0:1024], convw[:, gi, 0:1], None, ALU.mult),
                r=[mxkey, "convw"], w=["cvac"])
            for j in range(1, 4):
                p.add("vector", lambda e, mx=mx, gi=gi, j=j: e.scalar_tensor_tensor(
                    cvac[:], mx[:, j:j + 1024], convw[:, gi, j:j + 1], cvac[:],
                    ALU.mult, ALU.add),
                    r=[mxkey, "convw", "cvac"], w=["cvac"])
            p.add("vector", lambda e, mx=mx, gi=gi: e.tensor_copy(
                tails[:, gi, :], mx[:, 1024:1027]),
                r=[mxkey], w=["tails"])
            p.add("scalar", lambda e, m=m: e.activation(
                outb[m][:], cvac[:], AF.Silu),
                r=["cvac"], w=[okey[m]])
        if "qkv" in dbg_d and ps_i == 0 and hh == 0:
            for mi_, (buf_, bkey_) in enumerate([(qf, "qf"), (kf, "kf"),
                                                 (vf, "vf"), (zst, "zst")]):
                p.add("sync", lambda e, mi_=mi_, buf_=buf_: e.dma_start(
                    out=dbg_d["qkv"][mi_], in_=buf_[:]), r=[bkey_], dma=True)
        p.add("sync", lambda e: e.dma_start_transpose(qtm[:], qf[:]),
              r=["qf"], w=["qtm"], dma=True)
        p.add("sync", lambda e: e.dma_start_transpose(ktm[:], kf[:]),
              r=["kf"], w=["ktm"], dma=True)
        p.add("sync", lambda e: e.dma_start_transpose(vtm[:], vf[:]),
              r=["vf"], w=["vtm"], dma=True)
        if "ktm" in dbg_d and ps_i == 0 and hh == 0:
            dbg_store("ktm", ktm[:].rearrange("p a b -> p (a b)"), ["ktm"])
        p.add("vector", lambda e, hh=hh: e.tensor_copy(
            hcols[:, 0, :], scpack[:, :, 24 + hh]),
            r=["scpack"], w=["hcols"])
        p.add("vector", lambda e, hh=hh: e.tensor_copy(
            hcols[:, 1, :], scpack[:, :, hh]),
            r=["scpack"], w=["hcols"])
        p.add("gpsimd", lambda e: e.tensor_mul(
            sq2[:].rearrange("p a b -> p (a b)"),
            ktm[:].rearrange("p a b -> p (a b)"),
            ktm[:].rearrange("p a b -> p (a b)")),
            r=["ktm"], w=["sq2"])
        p.add("vector", lambda e: e.reduce_sum(
            hcols[:, 2, :].rearrange("p (a b) -> p a b", b=1), sq2[:], axis=AX.X),
            r=["sq2"], w=["hcols"])
        p.add("gpsimd", lambda e: e.tensor_mul(
            sq2[:].rearrange("p a b -> p (a b)"),
            qtm[:].rearrange("p a b -> p (a b)"),
            qtm[:].rearrange("p a b -> p (a b)")),
            r=["qtm"], w=["sq2"])
        p.add("vector", lambda e: e.reduce_sum(
            hcols[:, 3, :].rearrange("p (a b) -> p a b", b=1), sq2[:], axis=AX.X),
            r=["sq2"], w=["hcols"])
        p.add("scalar", lambda e: e.activation(
            hcols[:, 2, :], hcols[:, 2, :], AF.Ln, bias=cepsc[:, 1:2]),
            r=["hcols", "c_epsc"], w=["hcols"])
        p.add("scalar", lambda e: e.activation(
            hcols[:, 3, :], hcols[:, 3, :], AF.Ln, bias=cepsc[:, 1:2]),
            r=["hcols", "c_epsc"], w=["hcols"])
        p.add("vector", lambda e: e.tensor_sub(
            hcols[:, 4, :], hcols[:, 0, :], hcols[:, 1, :]),
            r=["hcols"], w=["hcols"])
        p.add("vector", lambda e: e.scalar_tensor_tensor(
            hcols[:, 4, :], hcols[:, 2, :], -0.5, hcols[:, 4, :],
            ALU.mult, ALU.add),
            r=["hcols"], w=["hcols"])
        p.add("vector", lambda e: e.scalar_tensor_tensor(
            hcols[:, 5, :], hcols[:, 2, :], 0.5, hcols[:, 0, :],
            ALU.mult, ALU.add),
            r=["hcols"], w=["hcols"])
        p.add("vector", lambda e: e.scalar_tensor_tensor(
            hcols[:, 6, :], hcols[:, 3, :], -0.5, hcols[:, 0, :],
            ALU.mult, ALU.add),
            r=["hcols"], w=["hcols"])
        p.add("vector", lambda e: e.tensor_scalar_add(
            hcols[:, 6, :], hcols[:, 6, :], LNDK),
            r=["hcols"], w=["hcols"])
        p.add("scalar", lambda e: e.activation(
            hcols[:, 7, :], hcols[:, 1, :], AF.Exp, scale=-1.0),
            r=["hcols"], w=["hcols"])
        p.add("scalar", lambda e: e.activation(
            hcols[:, 8, :], hcols[:, 4, :], AF.Exp),
            r=["hcols"], w=["hcols"])
        p.add("vector", lambda e: e.tensor_scalar_mul(
            hcols[:, 8, :], hcols[:, 8, :], -1.0),
            r=["hcols"], w=["hcols"])
        p.add("scalar", lambda e: e.activation(
            hcols[:, 12, :], hcols[:, 6, :], AF.Exp),
            r=["hcols"], w=["hcols"])
        p.add("tensor", lambda e: e.matmul(
            ps[1][:, 384:384 + CPP], ce127[:], hcols[:, 0, :],
            start=True, stop=True),
            r=["c_e127", "hcols"], w=["ps1"])
        p.add("vector", lambda e: e.tensor_copy(
            hcols[:, 10, :], ps[1][:, 384:384 + CPP]),
            r=["ps1"], w=["hcols"])
        p.add("scalar", lambda e: e.activation(
            hcols[:, 11, :], hcols[:, 10, :], AF.Exp),
            r=["hcols"], w=["hcols"])
        p.add("vector", lambda e: e.tensor_sub(
            hcols[:, 9, :], hcols[:, 10, :], hcols[:, 5, :]),
            r=["hcols"], w=["hcols"])
        p.add("scalar", lambda e: e.activation(
            hcols[:, 9, :], hcols[:, 9, :], AF.Exp),
            r=["hcols"], w=["hcols"])
        for ck in range(CPP):
            emit_instance(ps_i, hh, ck)
        p.add("sync", lambda e: e.dma_start_transpose(ofm[:], ogtm[:].rearrange("p a b -> p (a b)")),
              r=["ogtm"], w=["ofm"], dma=True)
        p.add("sync", lambda e, hh=hh, ps_i=ps_i: e.dma_start(
            out=cc_in[ps_i, hh],
            in_=ofm[:].rearrange("p a b -> p (a b)")),
            r=["ofm"], w=["cc_in"], dma=True)

    def emit_pass(ps_i):
        t0 = ps_i * PASS_T
        for st in range(8):
            xb = xld[st % 2]
            key = f"xld{st % 2}"
            rows = slice(t0 + st * 128, t0 + st * 128 + 128)
            p.add("sync", lambda e, rows=rows, xb=xb: e.dma_start(
                out=xb[:], in_=x_d[rows, :]), w=[key], dma=True)
            for j in range(4):
                p.add("vector", lambda e, xb=xb, j=j: e.bn_stats(
                    bst[:, j, :], xb[:, j * 512:(j + 1) * 512]),
                    r=[key], w=["bst"], special=True)
            p.add("vector", lambda e: e.bn_aggr(
                lnc_[:, 0:2], bst[:].rearrange("p a b -> p (a b)")),
                r=["bst"], w=["lnc"], special=True)
            p.add("scalar", lambda e: e.activation(
                lnc_[:, 1:2], lnc_[:, 1:2], AF.Ln, bias=cepsc[:, 0:1]),
                r=["lnc", "c_epsc"], w=["lnc"])
            p.add("scalar", lambda e: e.activation(
                lnc_[:, 1:2], lnc_[:, 1:2], AF.Exp, scale=-0.5),
                r=["lnc"], w=["lnc"])
            p.add("vector", lambda e, xb=xb: e.tensor_scalar(
                xtm[:], xb[:], lnc_[:, 0:1], lnc_[:, 1:2],
                ALU.subtract, ALU.mult),
                r=[key, "lnc"], w=["xtm"])
            p.add("sync", lambda e, st=st: e.dma_start_transpose(
                xn[:, :, st * 128:(st + 1) * 128], xtm[:]),
                r=["xtm"], w=["xn"], dma=True)
        if "xn" in dbg_d and ps_i == 0:
            dbg_store("xn", xn[:].rearrange("p a b -> p (a b)"), ["xn"])

        p.add("gpsimd", lambda e: e.dma_start(
            out=wbuf[0][:, :, 0:16],
            in_=wab_d.rearrange("(k q) c -> q k c", q=128)),
            w=["wbuf0"], dma=True)
        for n_ in range(2):
            for k in range(KCH):
                p.add("tensor", lambda e, n_=n_, k=k: e.matmul(
                    ps[0][0:16, 0:512], wbuf[0][:, k, 0:16],
                    xn[:, k, n_ * 512:(n_ + 1) * 512],
                    start=(k == 0), stop=(k == KCH - 1)),
                    r=["wbuf0", "xn"], w=["ps0"])
            p.add("vector", lambda e, n_=n_: e.tensor_copy(
                scf[:, 0, n_ * 512:(n_ + 1) * 512], ps[0][0:16, 0:512]),
                r=["ps0"], w=["scf"])
        p.add("vector", lambda e: e.tensor_scalar(
            scf[:, 1, :], scf[:, 0, :], csgn[:, 0:1], cbia[:, 0:1],
            ALU.mult, ALU.add),
            r=["scf", "c_sgn", "c_bia"], w=["scf"])
        p.add("scalar", lambda e: e.activation(scf[:, 0, :], scf[:, 1, :], AF.Abs),
              r=["scf"], w=["scf"])
        p.add("scalar", lambda e: e.activation(scf[:, 0, :], scf[:, 0, :],
                                               AF.Exp, scale=-1.0),
              r=["scf"], w=["scf"])
        p.add("vector", lambda e: e.tensor_scalar_add(scf[:, 0, :], scf[:, 0, :],
                                                      1.0),
              r=["scf"], w=["scf"])
        p.add("scalar", lambda e: e.activation(scf[:, 0, :], scf[:, 0, :], AF.Ln),
              r=["scf"], w=["scf"])
        p.add("vector", lambda e: e.tensor_relu(scf[:, 1, :], scf[:, 1, :]),
              r=["scf"], w=["scf"])
        p.add("vector", lambda e: e.tensor_add(scf[:, 0, :], scf[:, 0, :],
                                               scf[:, 1, :]),
              r=["scf"], w=["scf"])
        p.add("vector", lambda e: e.tensor_scalar(
            scG[:], scf[:, 0, :], negal[:, 0:1], None, ALU.mult),
            r=["scf", "negal"], w=["scG"])
        for ck in range(CPP):
            p.add("vector", lambda e, ck=ck: e.tensor_tensor_scan(
                scB[:, ck * C:(ck + 1) * C], cones[0:16, 0:128],
                scG[:, ck * C:(ck + 1) * C], 0.0, ALU.mult, ALU.add),
                r=["scG", "c_ones"], w=["scB"], special=True)
        if "sc" in dbg_d and ps_i == 0:
            dbg_store("sc", scG[:], ["scG"])
        if "scB" in dbg_d and ps_i == 0:
            dbg_store("scB", scB[:], ["scB"])
        for ck in range(CPP):
            p.add("tensor", lambda e, ck=ck: e.transpose(
                ps[1][:, 0:16], scG[:, ck * C:(ck + 1) * C], cidf[0:16, 0:16]),
                r=["scG", "c_idf"], w=["ps1"])
            p.add("vector", lambda e, ck=ck: e.tensor_copy(
                scpack[:, ck, 0:16], ps[1][:, 0:16]),
                r=["ps1"], w=["scpack"])
            p.add("tensor", lambda e, ck=ck: e.transpose(
                ps[1][:, 16:32], scB[:, ck * C:(ck + 1) * C], cidf[0:16, 0:16]),
                r=["scB", "c_idf"], w=["ps1"])
            p.add("vector", lambda e, ck=ck: e.tensor_copy(
                scpack[:, ck, 16:32], ps[1][:, 16:32]),
                r=["ps1"], w=["scpack"])
        for hh in range(HG):
            emit_head(ps_i, hh)

    for ps_i in range(NPASS):
        emit_pass(ps_i)

    p.add("sync", lambda e: e.dma_start(out=gus[:], in_=gus_d),
          w=["c_gus"], dma=True)
    p.add("sync", lambda e: e.dma_start(out=dns[:], in_=dns_d),
          w=["c_dns"], dma=True)
    p.add("gpsimd", lambda e: e.collective_compute(
        "AllGather", ALU.bypass, RG, [cc_in], [cc_out]),
        r=["cc_in"], w=["cc_out"], special=True)
    _pid = {}

    def _ofm_load(e, r_, h_):
        if "th" not in _pid:
            _pid["th"] = e.partition_id() % 2
        th = _pid["th"]
        src5 = cc_out[r_, :, h_, :, :]
        return e.dma_start(
            out=ofm_t[:, r_ * HG + h_, :],
            in_=src5[bass.ts(th, 1)].rearrange("a p t -> p (a t)"))
    for r_ in range(2):
        for h_ in range(HG):
            p.add("sync", lambda e, r_=r_, h_=h_: _ofm_load(e, r_, h_),
                  r=["cc_out"], w=["ofm_t"], dma=True)

    def ln_stats(srckey, src_ap):
        for j in range(4):
            p.add("vector", lambda e, j=j, src_ap=src_ap: e.bn_stats(
                bst[:, j, :], src_ap[:, j * 512:(j + 1) * 512]),
                r=[srckey], w=["bst"], special=True)
        p.add("vector", lambda e: e.bn_aggr(
            lnc_[:, 0:2], bst[:].rearrange("p a b -> p (a b)")),
            r=["bst"], w=["lnc"], special=True)
        p.add("scalar", lambda e: e.activation(
            lnc_[:, 1:2], lnc_[:, 1:2], AF.Ln, bias=cepsc[:, 0:1]),
            r=["lnc", "c_epsc"], w=["lnc"])
        p.add("scalar", lambda e: e.activation(
            lnc_[:, 1:2], lnc_[:, 1:2], AF.Exp, scale=-0.5),
            r=["lnc"], w=["lnc"])

    def ln_residual(src, srckey, make_xn1):
        for st in range(8):
            rows = slice(st * 128, st * 128 + 128)
            res_dram = xh_d if make_xn1 else x1s_d
            xb = xld[st % 2]
            xkey = f"xld{st % 2}"
            p.add("sync", lambda e, rows=rows, xb=xb, res_dram=res_dram:
                  e.dma_start(out=xb[:], in_=res_dram[rows, :]),
                  w=[xkey], dma=True)
            for kc in range(KCH):
                p.add("sync", lambda e, st=st, kc=kc, src=src: e.dma_start_transpose(
                    atm[:, kc * 128:(kc + 1) * 128],
                    src[:, kc, st * 128:(st + 1) * 128]),
                    r=[srckey], w=["atm"], dma=True)
            ln_stats("atm", atm[:])
            p.add("vector", lambda e: e.tensor_scalar(
                x1t[:], atm[:], lnc_[:, 0:1], lnc_[:, 1:2],
                ALU.subtract, ALU.mult),
                r=["atm", "lnc"], w=["x1t"])
            p.add("vector", lambda e, xb=xb: e.tensor_add(x1t[:], x1t[:], xb[:]),
                  r=["x1t", xkey], w=["x1t"])
            if make_xn1:
                p.add("sync", lambda e, rows=rows: e.dma_start(
                    out=x1s_d[rows, :], in_=x1t[:]), r=["x1t"], dma=True)
                if "x1" in dbg_d and st == 0:
                    p.add("sync", lambda e: e.dma_start(
                        out=dbg_d["x1"], in_=x1t[:]), r=["x1t"], dma=True)
                ln_stats("x1t", x1t[:])
                p.add("vector", lambda e: e.tensor_scalar(
                    x1n[:], x1t[:], lnc_[:, 0:1], lnc_[:, 1:2],
                    ALU.subtract, ALU.mult),
                    r=["x1t", "lnc"], w=["x1n"])
                p.add("sync", lambda e, st=st: e.dma_start_transpose(
                    xn[:, :, st * 128:(st + 1) * 128], x1n[:]),
                    r=["x1n"], w=["xn"], dma=True)
            else:
                p.add("sync", lambda e, rows=rows: e.dma_start(
                    out=out_d[rows, :], in_=x1t[:]), r=["x1t"], dma=True)

    for mg in range(4):
        wb = wbuf[mg % 2]
        wkey = f"wbuf{mg % 2}"
        p.add("gpsimd", lambda e, mg=mg, wb=wb: e.dma_start(
            out=wb[:],
            in_=wout_d.rearrange("(k q) c -> q k c", q=128)[:, :, mg * 512:(mg + 1) * 512]),
            w=[wkey], dma=True)
        for mt in range(4):
            m = mg * 4 + mt
            for n_ in range(2):
                bank = n_
                for k in range(KCH):
                    p.add("tensor", lambda e, wb=wb, mt=mt, n_=n_, k=k, bank=bank: e.matmul(
                        ps[bank][:, 0:512], wb[:, k, mt * 128:(mt + 1) * 128],
                        ofm_t[:, k, n_ * 512:(n_ + 1) * 512],
                        start=(k == 0), stop=(k == KCH - 1)),
                        r=[wkey, "ofm_t"], w=[f"ps{bank}"])
                p.add("vector", lambda e, m=m, n_=n_, bank=bank: e.tensor_copy(
                    afm[:, m, n_ * 512:(n_ + 1) * 512], ps[bank][:, 0:512]),
                    r=[f"ps{bank}"], w=["afm"])
    if "attn" in dbg_d:
        p.add("sync", lambda e: e.dma_start(
            out=dbg_d["attn"], in_=afm[:].rearrange("q k t -> q (k t)")),
            r=["afm"], dma=True)

    ln_residual(afm, "afm", True)

    for th2 in range(2):
        tc = slice(th2 * 512, th2 * 512 + 512)
        for mg in range(22):
            wb = wbuf[mg % 2]
            wkey = f"wbuf{mg % 2}"
            p.add("gpsimd", lambda e, mg=mg, wb=wb: e.dma_start(
                out=wb[:, :, 0:256],
                in_=wgu_d.rearrange("(k q) c -> q k c", q=128)[:, :, mg * 256:(mg + 1) * 256]),
                w=[wkey], dma=True)
            p.add("gpsimd", lambda e, mg=mg, wb=wb: e.dma_start(
                out=wb[:, :, 256:512],
                in_=wgu_d.rearrange("(k q) c -> q k c", q=128)[:, :, INTER + mg * 256: INTER + (mg + 1) * 256]),
                w=[wkey], dma=True)
            for mt in range(2):
                m = mg * 2 + mt
                for k in range(KCH):
                    p.add("tensor", lambda e, wb=wb, mt=mt, k=k, tc=tc: e.matmul(
                        ps[0][:, 0:512], wb[:, k, mt * 128:(mt + 1) * 128],
                        xn[:, k, tc], start=(k == 0), stop=(k == KCH - 1)),
                        r=[wkey, "xn"], w=["ps0"])
                for k in range(KCH):
                    p.add("tensor", lambda e, wb=wb, mt=mt, k=k, tc=tc: e.matmul(
                        ps[1][:, 0:512], wb[:, k, 256 + mt * 128: 256 + (mt + 1) * 128],
                        xn[:, k, tc], start=(k == 0), stop=(k == KCH - 1)),
                        r=[wkey, "xn"], w=["ps1"])
                p.add("scalar", lambda e, m=m: e.activation(
                    sqs[:, 0:512], ps[0][:, 0:512], AF.Silu,
                    scale=gus[:, m:m + 1]),
                    r=["ps0", "c_gus"], w=["sqs"])
                p.add("vector", lambda e, m=m: e.scalar_tensor_tensor(
                    hfm[:, m, :], ps[1][:, 0:512], gus[:, MI + m: MI + m + 1],
                    sqs[:, 0:512], ALU.mult, ALU.mult),
                    r=["ps1", "c_gus", "sqs"], w=["hfm"])
        if "hfm" in dbg_d and th2 == 0:
            p.add("sync", lambda e: e.dma_start(
                out=dbg_d["hfm"][:, 0:512], in_=hfm[:, 0, :]), r=["hfm"], dma=True)
        for mg in range(4):
            for kg in range(4):
                wb = wbuf[kg % 2]
                wkey = f"wbuf{kg % 2}"
                p.add("gpsimd", lambda e, mg=mg, kg=kg, wb=wb: e.dma_start(
                    out=wb[:, 0:11, :],
                    in_=wdn_d.rearrange("(k q) c -> q k c", q=128)[:, kg * 11:(kg + 1) * 11, mg * 512:(mg + 1) * 512]),
                    w=[wkey], dma=True)
                for mt in range(4):
                    bank = 4 + mt
                    for kk in range(11):
                        kglob = kg * 11 + kk
                        p.add("tensor", lambda e, wb=wb, mt=mt, kk=kk, kglob=kglob, bank=bank, kg=kg: e.matmul(
                            ps[bank][:, 0:512], wb[:, kk, mt * 128:(mt + 1) * 128],
                            hfm[:, kglob, :],
                            start=(kg == 0 and kk == 0),
                            stop=(kg == 3 and kk == 10)),
                            r=[wkey, "hfm"], w=[f"ps{bank}"])
            for mt in range(4):
                m = mg * 4 + mt
                bank = 4 + mt
                p.add("vector", lambda e, m=m, bank=bank, tc=tc: e.tensor_scalar(
                    ofm_t[:, m, tc], ps[bank][:, 0:512], dns[:, m:m + 1], None,
                    ALU.mult),
                    r=[f"ps{bank}", "c_dns"], w=["ofm_t"])

    ln_residual(ofm_t, "ofm_t", False)

    with nc.Block() as block:
        p.finalize(block)
    return nc, sorted(dbg_d)




def build_tail(dbg=()):
    nc = bass.Bass("TRN2", target_bir_lowering=False, debug=False,
                   detect_race_conditions=False)
    p = Prog(nc)
    TH = 1024

    o_d = nc.dram_tensor("o", [H, 128, TH], BF16, kind="ExternalInput").ap()
    xh_d = nc.dram_tensor("xh", [TH, D], F32, kind="ExternalInput").ap()
    wout_d = nc.dram_tensor("wout", [D, D], F32, kind="ExternalInput").ap()
    wgu_d = nc.dram_tensor("wgu", [D, 2 * INTER], F32, kind="ExternalInput").ap()
    wdn_d = nc.dram_tensor("wdn", [INTER, D], F32, kind="ExternalInput").ap()
    gus_d = nc.dram_tensor("gus", [128, 2 * MI], F32, kind="ExternalInput").ap()
    dns_d = nc.dram_tensor("dns", [128, KCH], F32, kind="ExternalInput").ap()
    epsc_d = nc.dram_tensor("epsc", [128, 4], F32, kind="ExternalInput").ap()
    out_d = nc.dram_tensor("out", [TH, D], F32, kind="ExternalOutput").ap()
    x1s_d = nc.dram_tensor("x1spill", [TH, D], F32, kind="Internal").ap()

    dbg_d = {}
    for nm, shp, dt in [("attn", [128, KCH * TH], BF16),
                        ("x1", [128, D], F32),
                        ("hfm", [128, TH], BF16)]:
        if nm in dbg:
            dbg_d[nm] = nc.dram_tensor("dbg_" + nm, shp, dt,
                                       kind="ExternalOutput").ap()

    Al = nc.alloc_sbuf_tensor
    ofm = Al("ofm", [128, H, TH], BF16)
    wbuf = Al("wbuf", [128, KCH, 512], BF16)
    afm = Al("afm", [128, KCH, TH], BF16)
    atm = Al("atm", [128, D], BF16)
    xld = Al("xld", [128, D], F32)
    x1t = Al("x1t", [128, D], F32)
    x1n = Al("x1n", [128, D], BF16)
    xn1 = Al("xn1", [128, KCH, TH], BF16)
    hfm = Al("hfm", [128, MI, 512], BF16)
    lnc_ = Al("lncols", [128, 4], F32)
    bst = Al("bnst", [128, 4, 6], F32)
    sqs = Al("sqs", [128, 512], F32)
    gus = Al("c_gus", [128, 2 * MI], F32)
    dns = Al("c_dns", [128, KCH], F32)
    cepsc = Al("c_epsc", [128, 4], F32)
    ps = [nc.alloc_psum_tensor(f"ps{i}", [128, 512], F32) for i in range(8)]

    def ln_stats(srckey, src_ap):
        for j in range(4):
            p.add("vector", lambda e, j=j: e.bn_stats(
                bst[:, j, :], src_ap[:, j * 512:(j + 1) * 512]),
                r=[srckey], w=["bst"], special=True)
        p.add("vector", lambda e: e.bn_aggr(
            lnc_[:, 0:2], bst[:].rearrange("p a b -> p (a b)")),
            r=["bst"], w=["lnc"], special=True)
        p.add("scalar", lambda e: e.activation(
            lnc_[:, 1:2], lnc_[:, 1:2], AF.Ln, bias=cepsc[:, 0:1]),
            r=["lnc", "c_epsc"], w=["lnc"])
        p.add("scalar", lambda e: e.activation(
            lnc_[:, 1:2], lnc_[:, 1:2], AF.Exp, scale=-0.5), r=["lnc"], w=["lnc"])

    def ln_residual(srckey, make_xn1):
        for st in range(8):
            rows = slice(st * 128, st * 128 + 128)
            res_dram = xh_d if make_xn1 else x1s_d
            p.add("sync", lambda e, rows=rows, res_dram=res_dram: e.dma_start(
                out=xld[:], in_=res_dram[rows, :]), w=["xld"], dma=True)
            for kc in range(KCH):
                p.add("sync", lambda e, st=st, kc=kc: e.dma_start_transpose(
                    atm[:, kc * 128:(kc + 1) * 128],
                    afm[:, kc, st * 128:(st + 1) * 128]),
                    r=[srckey], w=["atm"], dma=True)
            ln_stats("atm", atm[:])
            p.add("vector", lambda e: e.tensor_scalar(
                x1t[:], atm[:], lnc_[:, 0:1], lnc_[:, 1:2],
                ALU.subtract, ALU.mult),
                r=["atm", "lnc"], w=["x1t"])
            p.add("vector", lambda e: e.tensor_add(x1t[:], x1t[:], xld[:]),
                  r=["x1t", "xld"], w=["x1t"])
            if make_xn1:
                p.add("sync", lambda e, rows=rows: e.dma_start(
                    out=x1s_d[rows, :], in_=x1t[:]), r=["x1t"], dma=True)
                if "x1" in dbg_d and st == 0:
                    p.add("sync", lambda e: e.dma_start(
                        out=dbg_d["x1"], in_=x1t[:]), r=["x1t"], dma=True)
                ln_stats("x1t", x1t[:])
                p.add("vector", lambda e: e.tensor_scalar(
                    x1n[:], x1t[:], lnc_[:, 0:1], lnc_[:, 1:2],
                    ALU.subtract, ALU.mult),
                    r=["x1t", "lnc"], w=["x1n"])
                p.add("sync", lambda e, st=st: e.dma_start_transpose(
                    xn1[:, :, st * 128:(st + 1) * 128], x1n[:]),
                    r=["x1n"], w=["xn1"], dma=True)
            else:
                p.add("sync", lambda e, rows=rows: e.dma_start(
                    out=out_d[rows, :], in_=x1t[:]), r=["x1t"], dma=True)

    p.add("sync", lambda e: e.dma_start(
        out=ofm[:], in_=o_d.rearrange("h q t -> q h t")), w=["ofm"], dma=True)
    p.add("sync", lambda e: e.dma_start(out=gus[:], in_=gus_d), w=["gus"], dma=True)
    p.add("sync", lambda e: e.dma_start(out=dns[:], in_=dns_d), w=["dns"], dma=True)
    p.add("sync", lambda e: e.dma_start(out=cepsc[:], in_=epsc_d), w=["c_epsc"], dma=True)
    p.add("vector", lambda e: e.memset(bst[:], 0.0), w=["bst"])
    p.add("vector", lambda e: e.bn_stats(bst[:, 0, :], bst[:, 2, 0:6]),
          r=["bst"], w=["bst", "warm"])
    p.add("vector", lambda e: e.bn_aggr(bst[:, 1, 0:2], bst[:, 0, :]),
          r=["bst"], w=["bst", "warm"])

    for mg in range(4):
        p.add("gpsimd", lambda e, mg=mg: e.dma_start(
            out=wbuf[:],
            in_=wout_d.rearrange("(k q) c -> q k c", q=128)[:, :, mg * 512:(mg + 1) * 512]),
            w=["wbuf"], dma=True)
        for mt in range(4):
            m = mg * 4 + mt
            for n_ in range(2):
                bank = n_
                for k in range(KCH):
                    p.add("tensor", lambda e, mt=mt, n_=n_, k=k, bank=bank: e.matmul(
                        ps[bank][:, 0:512], wbuf[:, k, mt * 128:(mt + 1) * 128],
                        ofm[:, k, n_ * 512:(n_ + 1) * 512],
                        start=(k == 0), stop=(k == KCH - 1)),
                        r=["wbuf", "ofm"], w=[f"ps{bank}"])
                p.add("vector", lambda e, m=m, n_=n_, bank=bank: e.tensor_copy(
                    afm[:, m, n_ * 512:(n_ + 1) * 512], ps[bank][:, 0:512]),
                    r=[f"ps{bank}"], w=["afm"])
    if "attn" in dbg_d:
        p.add("sync", lambda e: e.dma_start(
            out=dbg_d["attn"], in_=afm[:].rearrange("q k t -> q (k t)")),
            r=["afm"], dma=True)

    ln_residual("afm", True)

    for th2 in range(2):
        tc = slice(th2 * 512, th2 * 512 + 512)
        for mg in range(22):
            p.add("gpsimd", lambda e, mg=mg: e.dma_start(
                out=wbuf[:, :, 0:256],
                in_=wgu_d.rearrange("(k q) c -> q k c", q=128)[:, :, mg * 256:(mg + 1) * 256]),
                w=["wbuf"], dma=True)
            p.add("gpsimd", lambda e, mg=mg: e.dma_start(
                out=wbuf[:, :, 256:512],
                in_=wgu_d.rearrange("(k q) c -> q k c", q=128)[:, :, INTER + mg * 256: INTER + (mg + 1) * 256]),
                w=["wbuf"], dma=True)
            for mt in range(2):
                m = mg * 2 + mt
                for k in range(KCH):
                    p.add("tensor", lambda e, mt=mt, k=k, tc=tc: e.matmul(
                        ps[0][:, 0:512], wbuf[:, k, mt * 128:(mt + 1) * 128],
                        xn1[:, k, tc], start=(k == 0), stop=(k == KCH - 1)),
                        r=["wbuf", "xn1"], w=["ps0"])
                for k in range(KCH):
                    p.add("tensor", lambda e, mt=mt, k=k, tc=tc: e.matmul(
                        ps[1][:, 0:512], wbuf[:, k, 256 + mt * 128: 256 + (mt + 1) * 128],
                        xn1[:, k, tc], start=(k == 0), stop=(k == KCH - 1)),
                        r=["wbuf", "xn1"], w=["ps1"])
                p.add("scalar", lambda e, m=m: e.activation(
                    sqs[:, 0:512], ps[0][:, 0:512], AF.Silu,
                    scale=gus[:, m:m + 1]),
                    r=["ps0", "gus"], w=["sqs"])
                p.add("vector", lambda e, m=m: e.scalar_tensor_tensor(
                    hfm[:, m, :], ps[1][:, 0:512], gus[:, MI + m: MI + m + 1],
                    sqs[:, 0:512], ALU.mult, ALU.mult),
                    r=["ps1", "gus", "sqs"], w=["hfm"])
        if "hfm" in dbg_d and th2 == 0:
            p.add("sync", lambda e: e.dma_start(
                out=dbg_d["hfm"][:, 0:512], in_=hfm[:, 0, :]), r=["hfm"], dma=True)
        for mg in range(4):
            for kg in range(4):
                p.add("gpsimd", lambda e, mg=mg, kg=kg: e.dma_start(
                    out=wbuf[:, 0:11, :],
                    in_=wdn_d.rearrange("(k q) c -> q k c", q=128)[:, kg * 11:(kg + 1) * 11, mg * 512:(mg + 1) * 512]),
                    w=["wbuf"], dma=True)
                for mt in range(4):
                    bank = 4 + mt
                    for kk in range(11):
                        kglob = kg * 11 + kk
                        p.add("tensor", lambda e, mt=mt, kk=kk, kglob=kglob, bank=bank, kg=kg: e.matmul(
                            ps[bank][:, 0:512], wbuf[:, kk, mt * 128:(mt + 1) * 128],
                            hfm[:, kglob, :],
                            start=(kg == 0 and kk == 0),
                            stop=(kg == 3 and kk == 10)),
                            r=["wbuf", "hfm"], w=[f"ps{bank}"])
            for mt in range(4):
                m = mg * 4 + mt
                bank = 4 + mt
                p.add("vector", lambda e, m=m, bank=bank, tc=tc: e.tensor_scalar(
                    afm[:, m, tc], ps[bank][:, 0:512], dns[:, m:m + 1], None,
                    ALU.mult),
                    r=[f"ps{bank}", "dns"], w=["afm"])

    ln_residual("afm", False)

    with nc.Block() as block:
        p.finalize(block)
    return nc, sorted(dbg_d)


def _consts():
    identf = np.eye(128, dtype=np.float32)
    ibf = np.eye(128, dtype=BF)
    onesf = np.ones((128, 128), np.float32)
    e127 = np.zeros((128, 128), np.float32)
    e127[127, :] = 1.0
    ii = np.arange(128)
    m_strictlow = (ii[:, None] > ii[None, :]).astype(np.float32)
    m_strictup = (ii[:, None] < ii[None, :]).astype(np.float32)
    m_upincl = (ii[:, None] <= ii[None, :]).astype(np.float32)
    maskm = np.ascontiguousarray(
        np.concatenate([m_strictlow, m_strictup, m_upincl], axis=1))
    maska = np.ascontiguousarray((1.0 - maskm) * FILL)
    epsc = np.zeros((128, 4), np.float32)
    epsc[:, 0] = LNEPS
    epsc[:, 1] = EPS6
    epsc[:, 2] = 1.0
    return identf, ibf, onesf, e127, maskm, maska, epsc


def kernel(**inputs):
    xfull = np.ascontiguousarray(np.asarray(inputs["L0_attn_hidden_states"],
                                            np.float32))
    Wqkv = np.asarray(inputs["L0_attn_W_qkv"], np.float32)
    convw = np.asarray(inputs["L0_attn_conv_w"], np.float32)
    Wz = np.asarray(inputs["L0_attn_W_z"], np.float32)
    Wb = np.asarray(inputs["L0_attn_W_b"], np.float32)
    Wa = np.asarray(inputs["L0_attn_W_a"], np.float32)
    dtb = np.asarray(inputs["L0_attn_dt_bias"], np.float32)
    Alog = np.asarray(inputs["L0_attn_A_log"], np.float32)
    nw = np.asarray(inputs["L0_attn_norm_weight"], np.float32)
    Wout = np.ascontiguousarray(np.asarray(inputs["L0_attn_W_out"], np.float32))
    gw = np.asarray(inputs["L0_ffn_gate_w"], np.float32)
    uw = np.asarray(inputs["L0_ffn_up_w"], np.float32)
    dw = np.asarray(inputs["L0_ffn_down_w"], np.float32)
    gs = np.asarray(inputs["L0_ffn_gate_s"], np.float32)
    us = np.asarray(inputs["L0_ffn_up_s"], np.float32)
    ds = np.asarray(inputs["L0_ffn_down_s"], np.float32)

    identf, ibf, onesf, e127, maskm, maska, epsc = _consts()
    nwt = np.ascontiguousarray(
        np.broadcast_to(nw[None, :], (128, 128)).astype(np.float32))
    wgu = np.ascontiguousarray(
        np.concatenate([gw.T, uw.T], axis=1).astype(np.float32))
    wdnT = np.ascontiguousarray(dw.T)
    gus = np.ascontiguousarray(np.concatenate(
        [gs.reshape(MI, 128).T, us.reshape(MI, 128).T], axis=1))
    dns = np.ascontiguousarray(ds.reshape(KCH, 128).T)

    in_maps = []
    for c in range(8):
        b, hg = c // 2, c % 2
        whead = np.empty((D, HG * 512), np.float32)
        conv_flat = np.empty((HG * 3 * 128, 4), np.float32)
        for h in range(HG):
            gh = hg * HG + h
            whead[:, h * 512 + 0:h * 512 + 128] = Wqkv[:, gh * 128:(gh + 1) * 128]
            whead[:, h * 512 + 128:h * 512 + 256] = Wqkv[:, 2048 + gh * 128: 2048 + (gh + 1) * 128]
            whead[:, h * 512 + 256:h * 512 + 384] = Wqkv[:, 4096 + gh * 128: 4096 + (gh + 1) * 128]
            whead[:, h * 512 + 384:h * 512 + 512] = Wz[:, gh * 128:(gh + 1) * 128]
            for m, base in enumerate((0, 2048, 4096)):
                conv_flat[(h * 3 + m) * 128:(h * 3 + m + 1) * 128, :] = \
                    convw[base + gh * 128: base + (gh + 1) * 128, 0, :]
        wab = np.ascontiguousarray(np.concatenate(
            [Wb[:, hg * 8:hg * 8 + 8], Wa[:, hg * 8:hg * 8 + 8]], axis=1))
        spsign = np.array([[-1.0]] * 8 + [[1.0]] * 8, np.float32)
        spbias = np.concatenate([np.zeros((8, 1), np.float32),
                                 dtb[hg * 8:hg * 8 + 8, None]], axis=0)
        alog8 = np.concatenate([np.zeros((8, 1), np.float32),
                                Alog[hg * 8:hg * 8 + 8, None]], axis=0)
        sl = slice(hg * 1024, hg * 1024 + 1024)
        in_maps.append(dict(
            x=xfull[b], whead=np.ascontiguousarray(whead),
            convw=conv_flat, wab=wab, spsign=spsign,
            spbias=np.ascontiguousarray(spbias), alog=alog8, nw=nwt,
            identf=identf, ibf=ibf, onesf=onesf, e127=e127,
            maskm=maskm, maska=maska, epsc=epsc,
            xh=np.ascontiguousarray(xfull[b, sl]), wout=Wout,
            wgu=wgu, wdn=wdnT, gus=gus, dns=dns))

    nc, _ = build_fused()
    res = run_bass_kernel_spmd(nc, in_maps, core_ids=list(range(8)))

    out = np.empty((B, T, D), np.float32)
    for c in range(8):
        b, th = c // 2, c % 2
        out[b, th * 1024:(th + 1) * 1024] = np.asarray(res.results[c]["out"],
                                                       np.float32)
    return out


# version 05
# speedup vs baseline: 1.3642x; 1.1799x over previous
import math
import numpy as np
import ml_dtypes
import concourse.bass as bass
import concourse.mybir as mybir
from concourse.bass_utils import run_bass_kernel_spmd

F32 = mybir.dt.float32
BF16 = mybir.dt.bfloat16
AF = mybir.ActivationFunctionType
ALU = mybir.AluOpType
AX = mybir.AxisListType
BF = ml_dtypes.bfloat16

B, T, D = 4, 2048, 2048
H, DK, DV = 16, 128, 128
HG = 8
KCH = 16
C = 128
PASS_T = 1024
NPASS = 2
CPP = PASS_T // C
INTER = 5632
MI = INTER // 128
FILL = -30000.0
LNEPS = 1e-5
EPS6 = 1e-6
LNDK = -0.5 * math.log(DK)


class Prog:

    ENGS = ("sync", "gpsimd", "vector", "scalar", "tensor")

    def __init__(self, nc):
        self.nc = nc
        self.ops = []

    def add(self, eng, emit, r=(), w=(), dma=False, special=False):
        if special:
            r = tuple(r) + ("warm",)
        self.ops.append((eng, emit, tuple(r), tuple(w), bool(dma), bool(special)))

    def finalize(self, block):
        n = len(self.ops)
        deps = [set() for _ in range(n)]
        last_w = {}
        readers = {}
        for i, (eng, emit, reads, writes, dma, special) in enumerate(self.ops):
            for b in reads:
                if b in last_w:
                    deps[i].add(last_w[b])
            for b in writes:
                if b in last_w:
                    deps[i].add(last_w[b])
                for rd in readers.get(b, ()):
                    deps[i].add(rd)
            for b in reads:
                readers.setdefault(b, []).append(i)
            for b in writes:
                last_w[b] = i
                readers[b] = []
        UNSAFE = ("vector", "gpsimd")
        need = [False] * n
        for i in range(n):
            eng, _, _, _, dma, special = self.ops[i]
            if dma:
                need[i] = True
            for d in deps[i]:
                if (self.ops[d][4] or self.ops[d][0] != eng or special
                        or self.ops[d][5] or eng in UNSAFE):
                    need[d] = True
        sems = {}
        for e in self.ENGS:
            sems[(e, False)] = self.nc.alloc_semaphore(f"sb_{e}_c")
            sems[(e, True)] = self.nc.alloc_semaphore(f"sb_{e}_d")
        counter = {k: 0 for k in sems}
        val = [None] * n
        for i in range(n):
            if need[i]:
                k = (self.ops[i][0], self.ops[i][4])
                counter[k] += 16 if self.ops[i][4] else 1
                val[i] = (k, counter[k])
        by_eng = {e: [] for e in self.ENGS}
        for i, op in enumerate(self.ops):
            by_eng[op[0]].append(i)

        def mk_section(eng_name, idxs):
            def section(eng):
                wm = {}
                for i in idxs:
                    _, emit, _, _, dma, special = self.ops[i]
                    waits = {}
                    for d in deps[i]:
                        d_eng, d_dma, d_sp = (self.ops[d][0], self.ops[d][4],
                                              self.ops[d][5])
                        if (not d_dma and d_eng == eng_name
                                and not (special or d_sp)
                                and eng_name not in ("vector", "gpsimd")):
                            continue
                        k, v = val[d]
                        if waits.get(k, 0) < v:
                            waits[k] = v
                    for k in sorted(waits, key=str):
                        v = waits[k]
                        if wm.get(k, 0) < v:
                            eng.wait_ge(sems[k], v)
                            wm[k] = v
                    inst = emit(eng)
                    if need[i]:
                        k, v = val[i]
                        inst.then_inc(sems[k], 16 if k[1] else 1)
            return section

        for e in self.ENGS:
            if by_eng[e]:
                getattr(block, e)(mk_section(e, by_eng[e]))
        finals = [(k, v) for k, v in counter.items() if v > 0]

        def tail(eng):
            for k, v in sorted(finals, key=lambda kv: str(kv[0])):
                eng.wait_ge(sems[k], v)
        block.sync(tail)


RG = [[0, 1], [2, 3], [4, 5], [6, 7]]


def build_fused(dbg=()):
    nc = bass.Bass("TRN2", target_bir_lowering=False, debug=False,
                   detect_race_conditions=False)
    p = Prog(nc)
    TH = 1024

    x_d = nc.dram_tensor("x", [T, D], F32, kind="ExternalInput").ap()
    whead_d = nc.dram_tensor("whead", [D, HG * 512], F32, kind="ExternalInput").ap()
    convw_d = nc.dram_tensor("convw", [HG * 3 * 128, 4], F32, kind="ExternalInput").ap()
    wab_d = nc.dram_tensor("wab", [D, 16], F32, kind="ExternalInput").ap()
    spsign_d = nc.dram_tensor("spsign", [16, 1], F32, kind="ExternalInput").ap()
    spbias_d = nc.dram_tensor("spbias", [16, 1], F32, kind="ExternalInput").ap()
    alog_d = nc.dram_tensor("alog", [16, 1], F32, kind="ExternalInput").ap()
    nw_d = nc.dram_tensor("nw", [128, 128], F32, kind="ExternalInput").ap()
    identf_d = nc.dram_tensor("identf", [128, 128], F32, kind="ExternalInput").ap()
    ibf_d = nc.dram_tensor("ibf", [128, 128], BF16, kind="ExternalInput").ap()
    onesf_d = nc.dram_tensor("onesf", [128, 128], F32, kind="ExternalInput").ap()
    e127_d = nc.dram_tensor("e127", [128, 128], F32, kind="ExternalInput").ap()
    epsc_d = nc.dram_tensor("epsc", [128, 4], F32, kind="ExternalInput").ap()
    maskm_d = nc.dram_tensor("maskm", [128, 3 * 128], F32, kind="ExternalInput").ap()
    maska_d = nc.dram_tensor("maska", [128, 3 * 128], F32, kind="ExternalInput").ap()

    xh_d = nc.dram_tensor("xh", [TH, D], F32, kind="ExternalInput").ap()
    wout_d = nc.dram_tensor("wout", [D, D], F32, kind="ExternalInput").ap()
    wgu_d = nc.dram_tensor("wgu", [D, 2 * INTER], F32, kind="ExternalInput").ap()
    wdn_d = nc.dram_tensor("wdn", [INTER, D], F32, kind="ExternalInput").ap()
    gus_d = nc.dram_tensor("gus", [128, 2 * MI], F32, kind="ExternalInput").ap()
    dns_d = nc.dram_tensor("dns", [128, KCH], F32, kind="ExternalInput").ap()
    out_d = nc.dram_tensor("out", [TH, D], F32, kind="ExternalOutput").ap()
    x1s_d = nc.dram_tensor("x1spill", [TH, D], F32, kind="Internal").ap()
    cc_in = nc.dram_tensor("cc_in", [NPASS, HG, 128, TH], BF16,
                           kind="Internal").ap()
    cc_out = nc.dram_tensor("cc_out", [2, NPASS, HG, 128, TH], BF16,
                            kind="Internal").ap()
    dbg_d = {}
    for nm, shp, dt in [
        ("xn", [128, KCH * PASS_T], BF16), ("qkv", [4, 128, PASS_T], BF16),
        ("sc", [16, PASS_T], F32), ("scB", [16, PASS_T], F32), ("ktm", [128, CPP * 128], BF16),
        ("TT", [128, 128], BF16), ("U", [128, 128], BF16),
        ("S", [128, 128], F32), ("AL", [128, 128], BF16),
        ("expin", [128, 3 * 128], F32), ("hcols", [128, 14], F32),
        ("attn", [128, KCH * 1024], BF16), ("x1", [128, D], F32),
        ("hfm", [128, 1024], BF16),
    ]:
        if nm in dbg:
            dbg_d[nm] = nc.dram_tensor("dbg_" + nm, shp, dt,
                                       kind="ExternalOutput").ap()

    Al = nc.alloc_sbuf_tensor
    REGION = 98304
    base0, _ = nc.bump_sbuf(REGION, align_bytes=256)
    _cur = [0]

    def Ar(name, shape, dtype):
        import math as _m
        nbytes = _m.prod(shape[1:]) * (2 if dtype == BF16 else 4)
        off = (_cur[0] + 255) // 256 * 256
        _cur[0] = off + nbytes
        assert _cur[0] <= REGION, (name, _cur[0])
        return nc.alloc_sbuf_tensor_at(name, shape, dtype, offset=base0 + off)

    def ArAt(name, shape, dtype, off):
        return nc.alloc_sbuf_tensor_at(name, shape, dtype, offset=base0 + off)

    xn = Al("xn", [128, KCH, PASS_T], BF16)
    xld = [Al(f"xld{i}", [128, D], F32) for i in range(2)]
    xtm = Ar("xtm", [128, D], BF16)
    lnc_ = Al("lncols", [128, 4], F32)
    bst = Al("bnst", [128, 4, 6], F32)
    wbuf = [Al(f"wbuf{i}", [128, KCH, 512], BF16) for i in range(2)]
    convw = Al("convwt", [128, HG * 3, 4], F32)
    mixs = [Ar(f"mixs{i}", [128, 1027], F32) for i in range(2)]
    cvac = Ar("cvac", [128, 1024], F32)
    tails = Al("tails", [128, HG * 3, 3], F32)
    qf = Ar("qf", [128, PASS_T], BF16)
    kf = Ar("kf", [128, PASS_T], BF16)
    vf = Ar("vf", [128, PASS_T], BF16)
    zst = Ar("zst", [128, PASS_T], BF16)
    qtm = Ar("qtm", [128, CPP, 128], BF16)
    ktm = Ar("ktm", [128, CPP, 128], BF16)
    vtm = Ar("vtm", [128, CPP, 128], BF16)
    ztm = Ar("ztm", [128, HG, CPP, 128], BF16)
    ogtm = Ar("ogtm", [128, CPP, 128], BF16)
    ofm = Ar("ofm", [128, CPP, 128], BF16)
    scG = Al("scG", [16, PASS_T], F32)
    scf = Ar("scf", [16, 2, PASS_T], F32)
    scpack = Al("scpack", [128, CPP, 32], F32)
    hcols = Al("hcols", [128, 14, CPP], F32)
    rcol = Al("rcol", [128, 2], F32)
    colbc = [Ar(f"colbc{i}", [128, 128], F32) for i in range(2)]
    expin = [Ar(f"expin{i}", [128, 3, 128], F32) for i in range(2)]
    expout = [Ar(f"expout{i}", [128, 3, 128], BF16) for i in range(2)]
    AL_ = [Ar(f"tAL{i}", [128, 128], BF16) for i in range(2)]
    AU_ = [Ar(f"tAU{i}", [128, 128], BF16) for i in range(2)]
    AUn_ = [Ar(f"tAUn{i}", [128, 128], BF16) for i in range(2)]
    pw = {}
    for i in range(2):
        for lv in (2, 4, 8, 16, 32):
            pw[(lv, "L", i)] = Ar(f"A{lv}L{i}", [128, 128], BF16)
            pw[(lv, "U", i)] = Ar(f"A{lv}U{i}", [128, 128], BF16)
        pw[(64, "L", i)] = Ar(f"A64L{i}", [128, 128], BF16)
    X_ = [Ar(f"Xtt{i}", [128, 128], BF16) for i in range(2)]
    Rv = [Ar(f"Rv{i}", [128, 128], BF16) for i in range(2)]
    Rk = [Ar(f"Rk{i}", [128, 128], BF16) for i in range(2)]
    MTb = Ar("MTb", [128, CPP, 128], BF16)
    Ktb = Ar("Ktb", [128, CPP, 128], BF16)
    Wfmb = Ar("Wfmb", [128, CPP, 128], BF16)
    U0b = Ar("U0b", [128, CPP, 128], BF16)
    Utm = Ar("Utm", [128, 128], BF16)
    S_f = Ar("S_f", [128, HG, 128], F32)
    S_b = Ar("S_b", [128, 128], BF16)
    gts = Ar("gts", [128, 128], F32)
    sq2 = Ar("sq2", [128, CPP, 128], F32)
    cnw = Al("c_nw", [128, 128], F32)
    cidf = Al("c_identf", [128, 128], F32)
    cibf = Al("c_ibf", [128, 128], BF16)
    cones = Al("c_onesf", [128, 128], F32)
    ce127 = Al("c_e127", [128, 128], F32)
    cepsc = Al("c_epsc", [128, 4], F32)
    cmaskm = Al("c_maskm", [128, 3, 128], F32)
    cmaska = Al("c_maska", [128, 3, 128], F32)
    csgn = Al("c_spsign", [16, 1], F32)
    cbia = Al("c_spbias", [16, 1], F32)
    calog = Al("c_alog", [16, 1], F32)
    negal = Al("negal", [16, 1], F32)
    scB = Ar("scB", [16, PASS_T], F32)
    afm = ArAt("afm", [128, KCH, TH], BF16, 0)
    hfm = ArAt("hfm", [128, MI, 512], BF16, 0)
    ofm_t = ArAt("ofm_t", [128, H, TH], BF16, 45056)
    atm = ArAt("atm", [128, D], BF16, 77824)
    x1t = ArAt("x1t", [128, D], F32, 81920)
    x1n = ArAt("x1n", [128, D], BF16, 90112)
    sqs = ArAt("sqs", [128, 512], F32, 94208)
    gus = Al("c_gus", [128, 2 * MI], F32)
    dns = Al("c_dns", [128, KCH], F32)
    ps = [nc.alloc_psum_tensor(f"ps{i}", [128, 512], F32) for i in range(8)]

    def dbg_store(nm, src, rkeys):
        if nm in dbg_d:
            p.add("sync", lambda e, nm=nm, src=src: e.dma_start(
                out=dbg_d[nm], in_=src), r=rkeys, dma=True)

    for dst, key, dr in [(cnw, "c_nw", nw_d), (cidf, "c_idf", identf_d),
                         (cibf, "c_ibf", ibf_d), (cones, "c_ones", onesf_d),
                         (ce127, "c_e127", e127_d), (csgn, "c_sgn", spsign_d),
                         (cbia, "c_bia", spbias_d),
                         (cepsc, "c_epsc", epsc_d),
                         (cmaskm, "c_maskm", maskm_d),
                         (cmaska, "c_maska", maska_d)]:
        p.add("sync", lambda e, dst=dst, dr=dr: e.dma_start(
            out=dst[:] if len(dst.shape) == 2 else dst[:].rearrange("p a b -> p (a b)"),
            in_=dr), w=[key], dma=True)
    p.add("sync", lambda e: e.dma_start(out=calog[:], in_=alog_d),
          w=["c_alog"], dma=True)
    p.add("sync", lambda e: e.dma_start(
        out=convw[:], in_=convw_d.rearrange("(g q) k -> q g k", q=128)),
        w=["convw"], dma=True)
    p.add("scalar", lambda e: e.activation(negal[:], calog[:], AF.Exp),
          r=["c_alog"], w=["negal"])
    p.add("vector", lambda e: e.tensor_scalar(negal[:], negal[:], csgn[:, 0:1],
                                              -1.0, ALU.mult, ALU.mult),
          r=["negal", "c_sgn"], w=["negal"])
    p.add("vector", lambda e: e.memset(tails[:], 0.0), w=["tails"])
    p.add("vector", lambda e: e.memset(S_f[:], 0.0), w=["S_f"])
    p.add("vector", lambda e: e.memset(bst[:], 0.0), w=["bst"])
    p.add("vector", lambda e: e.bn_stats(bst[:, 0, :], bst[:, 2, 0:6]),
          r=["bst"], w=["bst", "warm"])
    p.add("vector", lambda e: e.bn_aggr(bst[:, 1, 0:2], bst[:, 0, :]),
          r=["bst"], w=["bst", "warm"])
    p.add("vector", lambda e: e.tensor_tensor_scan(
        bst[:, 2, 0:6], bst[:, 3, 0:6], bst[:, 2, 0:6], 0.0,
        ALU.mult, ALU.add),
        r=["bst"], w=["bst", "warm"])

    def pre_steps(ps_i, hh, ck):
        par = ck % 2
        P, Q = ps[par], ps[4 + par]
        kP, kQ = f"ps{par}", f"ps{4 + par}"
        cb, ei, eo = colbc[par], expin[par], expout[par]
        al, au, aun, xx = AL_[par], AU_[par], AUn_[par], X_[par]
        rv, rk = Rv[par], Rk[par]
        kcb, kei, keo = f"colbc{par}", f"expin{par}", f"expout{par}"
        kal, kau, kaun, kx = f"AL{par}", f"AU{par}", f"AUn{par}", f"X{par}"
        krv, krk = f"Rv{par}", f"Rk{par}"
        S = []

        def st(*ops):
            S.append(list(ops))
        ksl = lambda: kf[:, ck * 128:(ck + 1) * 128]
        qsl = lambda: qf[:, ck * 128:(ck + 1) * 128]
        st(("vector", lambda e: e.tensor_scalar(
            cb[:], cones[:], hcols[:, 5, ck:ck + 1], None, ALU.mult),
            ["c_ones", "hcols"], [kcb]))
        st(("tensor", lambda e: e.transpose(P[:, 256:384], cb[:], cidf[:]),
            [kcb, "c_idf"], [kP]))
        st(("vector", lambda e: e.tensor_scalar(
            ei[:, 0, :], P[:, 256:384], hcols[:, 4, ck:ck + 1], -1.0,
            ALU.subtract, ALU.mult), [kP, "hcols"], [kei]),
           ("vector", lambda e: e.tensor_scalar(
            cb[:], cones[:], hcols[:, 4, ck:ck + 1], None, ALU.mult),
            ["c_ones", "hcols"], [kcb]))
        st(("tensor", lambda e: e.transpose(P[:, 384:512], cb[:], cidf[:]),
            [kcb, "c_idf"], [kP]))
        st(("vector", lambda e: e.tensor_scalar(
            ei[:, 1, :], P[:, 384:512], hcols[:, 5, ck:ck + 1], None,
            ALU.subtract), [kP, "hcols"], [kei]),
           ("vector", lambda e: e.tensor_scalar(
            cb[:], cones[:], hcols[:, 6, ck:ck + 1], None, ALU.mult),
            ["c_ones", "hcols"], [kcb]))
        st(("tensor", lambda e: e.transpose(Q[:, 0:128], cb[:], cidf[:]),
            [kcb, "c_idf"], [kQ]))
        st(("vector", lambda e: e.tensor_scalar(
            ei[:, 2, :], Q[:, 0:128], hcols[:, 5, ck:ck + 1], None,
            ALU.subtract), [kQ, "hcols"], [kei]))
        st(("gpsimd", lambda e: e.tensor_mul(
            ei[:].rearrange("p a b -> p (a b)"),
            ei[:].rearrange("p a b -> p (a b)"),
            cmaskm[:].rearrange("p a b -> p (a b)")),
            [kei, "c_maskm"], [kei]))
        st(("gpsimd", lambda e: e.tensor_add(
            ei[:].rearrange("p a b -> p (a b)"),
            ei[:].rearrange("p a b -> p (a b)"),
            cmaska[:].rearrange("p a b -> p (a b)")),
            [kei, "c_maska"], [kei]))
        st(("scalar", lambda e: e.activation(
            eo[:].rearrange("p a b -> p (a b)"),
            ei[:].rearrange("p a b -> p (a b)"), AF.Exp),
            [kei], [keo]))
        st(("tensor", lambda e: e.matmul(P[:, 0:128], ksl(), ksl(),
                                         start=True, stop=True),
            ["kf"], [kP]),
           ("tensor", lambda e: e.matmul(P[:, 128:256], ksl(), qsl(),
                                         start=True, stop=True),
            ["kf", "qf"], [kP]))
        st(("vector", lambda e: e.tensor_mul(al[:], P[:, 0:128], eo[:, 0, :]),
            [kP, keo], [kal]),
           ("vector", lambda e: e.tensor_mul(au[:], P[:, 0:128], eo[:, 1, :]),
            [kP, keo], [kau]))
        st(("vector", lambda e: e.tensor_scalar_mul(aun[:], au[:], -1.0),
            [kau], [kaun]),
           ("vector", lambda e, ck=ck: e.tensor_mul(
            MTb[:, ck, :], P[:, 128:256], eo[:, 2, :]),
            [kP, keo], ["MTb"]))
        chain = [((2, "L"), kau, au, kal, al), ((2, "U"), kal, al, kau, au)]
        for lv in (4, 8, 16, 32):
            chain.append(((lv, "L"), f"A{lv//2}U{par}", pw[(lv // 2, "U", par)],
                          f"A{lv//2}L{par}", pw[(lv // 2, "L", par)]))
            chain.append(((lv, "U"), f"A{lv//2}L{par}", pw[(lv // 2, "L", par)],
                          f"A{lv//2}U{par}", pw[(lv // 2, "U", par)]))
        chain.append(((64, "L"), f"A32U{par}", pw[(32, "U", par)],
                      f"A32L{par}", pw[(32, "L", par)]))
        for idx, (key, lk, lhs, rk_, rhs) in enumerate(chain):
            reg = slice(128 + (idx % 2) * 128, 256 + (idx % 2) * 128)
            st(("tensor", lambda e, reg=reg, lhs=lhs, rhs=rhs: e.matmul(
                Q[:, reg], lhs[:], rhs[:], start=True, stop=True),
                [lk, rk_], [kQ]),
               ("scalar", lambda e, reg=reg, key=key: e.copy(
                pw[(key[0], key[1], par)][:], Q[:, reg]),
                [kQ], [f"A{key[0]}{key[1]}{par}"]))
        st(("vector", lambda e: e.tensor_add(xx[:], cibf[:], aun[:]),
            ["c_ibf", kaun], [kx]))
        for li, lv in enumerate((2, 4, 8, 16, 32, 64)):
            reg = slice(128 + (li % 2) * 128, 256 + (li % 2) * 128)
            st(("tensor", lambda e, reg=reg, lv=lv: e.matmul(
                Q[:, reg], pw[(lv, "L", par)][:], xx[:],
                start=True, stop=False),
                [f"A{lv}L{par}", kx], [kQ]),
               ("tensor", lambda e, reg=reg: e.matmul(
                Q[:, reg], cibf[:], xx[:], start=False, stop=True),
                ["c_ibf", kx], [kQ]))
            st(("scalar", lambda e, reg=reg: e.copy(xx[:], Q[:, reg]),
                [kQ], [kx]))
        st(("vector", lambda e: e.tensor_scalar(
            rv[:], vtm[:, ck, :], hcols[:, 7, ck:ck + 1], None, ALU.mult),
            ["vtm", "hcols"], [krv]),
           ("vector", lambda e: e.tensor_scalar(
            rk[:], ktm[:, ck, :], hcols[:, 8, ck:ck + 1], None, ALU.mult),
            ["ktm", "hcols"], [krk]),
           ("vector", lambda e, ck=ck: e.tensor_scalar(
            Ktb[:, ck, :], ktm[:, ck, :], hcols[:, 9, ck:ck + 1], None,
            ALU.mult), ["ktm", "hcols"], ["Ktb"]))
        st(("tensor", lambda e: e.matmul(Q[:, 384:512], rk[:], xx[:],
                                         start=True, stop=True),
            [krk, kx], [kQ]))
        st(("scalar", lambda e, ck=ck: e.copy(Wfmb[:, ck, :], Q[:, 384:512]),
            [kQ], ["Wfmb"]))
        st(("tensor", lambda e: e.matmul(Q[:, 384:512], xx[:], rv[:],
                                         start=True, stop=True),
            [kx, krv], [kQ]))
        st(("vector", lambda e, ck=ck: e.tensor_copy(U0b[:, ck, :],
                                                     Q[:, 384:512]),
            [kQ], ["U0b"]))
        if "TT" in dbg_d and ps_i == 0 and hh == 0 and ck == 1:
            st(("sync", lambda e: e.dma_start(out=dbg_d["TT"], in_=xx[:]),
                [kx], [], True))
            st(("sync", lambda e: e.dma_start(out=dbg_d["AL"], in_=al[:]),
                [kal], [], True))
            st(("sync", lambda e: e.dma_start(
                out=dbg_d["hcols"],
                in_=hcols[:, :, ck:ck + 1].rearrange("p a b -> p (a b)"),
                allow_slow_non_contiguous=True), ["hcols"], [], True))
        return S

    def emit_chain(ps_i, hh, ck):
        dbg_on = ("U" in dbg_d and ps_i == 0 and hh == 0 and ck == 1)
        p.add("scalar", lambda e: e.copy(S_b[:], S_f[:, hh, :]),
              r=["S_f"], w=["S_b"])
        p.add("tensor", lambda e, ck=ck: e.matmul(
            ps[6][:, 0:128], Wfmb[:, ck, :], S_b[:], start=True, stop=False),
            r=["Wfmb", "S_b"], w=["ps6"])
        p.add("tensor", lambda e, ck=ck: e.matmul(
            ps[6][:, 0:128], cibf[:], U0b[:, ck, :], start=False, stop=True),
            r=["c_ibf", "U0b"], w=["ps6"])
        p.add("vector", lambda e: e.tensor_copy(Utm[:], ps[6][:, 0:128]),
              r=["ps6"], w=["Utm"])
        if dbg_on:
            dbg_store("U", Utm[:], ["Utm"])
        p.add("tensor", lambda e, ck=ck: e.matmul(
            ps[7][:, 0:128], qf[:, ck * 128:(ck + 1) * 128], S_b[:],
            start=True, stop=True),
            r=["qf", "S_b"], w=["ps7"])
        p.add("tensor", lambda e, ck=ck: e.matmul(
            ps[7][:, 128:256], MTb[:, ck, :], Utm[:], start=True, stop=True),
            r=["MTb", "Utm"], w=["ps7"])
        p.add("tensor", lambda e, ck=ck: e.matmul(
            ps[6][:, 128:256], Ktb[:, ck, :], Utm[:], start=True, stop=True),
            r=["Ktb", "Utm"], w=["ps6"])
        p.add("vector", lambda e, ck=ck: e.scalar_tensor_tensor(
            S_f[:, hh, :], S_f[:, hh, :], hcols[:, 11, ck:ck + 1],
            ps[6][:, 128:256], ALU.mult, ALU.add),
            r=["S_f", "hcols", "ps6"], w=["S_f"])
        if "S" in dbg_d and ps_i == 0 and hh == 0 and ck == 1:
            dbg_store("S", S_f[:, 0, :], ["S_f"])
        p.add("scalar", lambda e, ck=ck: e.activation(
            gts[:], ps[7][:, 0:128], AF.Copy, scale=hcols[:, 12, ck:ck + 1]),
            r=["ps7", "hcols"], w=["gts"])
        p.add("vector", lambda e: e.tensor_add(gts[:], gts[:], ps[7][:, 128:256]),
              r=["gts", "ps7"], w=["gts"])
        p.add("vector", lambda e, ck=ck: e.tensor_mul(gts[:], gts[:],
                                                      ztm[:, hh, ck, :]),
              r=["gts", "ztm"], w=["gts"])
        p.add("vector", lambda e: e.tensor_mul(sq2[:, 0, :], gts[:], gts[:]),
              r=["gts"], w=["sq2"])
        p.add("vector", lambda e: e.reduce_sum(rcol[:, 0:1], sq2[:, 0, :],
                                               axis=AX.X),
              r=["sq2"], w=["rcol"])
        p.add("vector", lambda e: e.tensor_scalar(
            rcol[:, 0:1], rcol[:, 0:1], 1.0 / DV, EPS6, ALU.mult, ALU.add),
            r=["rcol"], w=["rcol"])
        p.add("scalar", lambda e: e.activation(rcol[:, 0:1], rcol[:, 0:1],
                                               AF.Ln),
              r=["rcol"], w=["rcol"])
        p.add("scalar", lambda e: e.activation(rcol[:, 0:1], rcol[:, 0:1],
                                               AF.Exp, scale=-0.5),
              r=["rcol"], w=["rcol"])
        p.add("vector", lambda e, ck=ck: e.scalar_tensor_tensor(
            ogtm[:, ck, :], gts[:], rcol[:, 0:1], cnw[:], ALU.mult, ALU.mult),
            r=["gts", "rcol", "c_nw"], w=["ogtm"])

    def emit_head(ps_i, hh):
        t0 = ps_i * PASS_T
        wb = wbuf[hh % 2]
        wkey = f"wbuf{hh % 2}"
        p.add("gpsimd", lambda e: e.dma_start(
            out=wb[:],
            in_=whead_d.rearrange("(k q) c -> q k c", q=128)[:, :, hh * 512:(hh + 1) * 512]),
            w=[wkey], dma=True)
        outb = {0: qf, 1: kf, 2: vf, 3: zst}
        okey = {0: "qf", 1: "kf", 2: "vf", 3: "zst"}
        for m in range(4):
            mx = mixs[m % 2]
            mxkey = f"mixs{m % 2}"
            for n_ in range(2):
                bank = 2 + n_
                bkey = f"ps{bank}"
                for k in range(KCH):
                    p.add("tensor", lambda e, m=m, n_=n_, k=k, bank=bank: e.matmul(
                        ps[bank][:, 0:512], wb[:, k, m * 128:(m + 1) * 128],
                        xn[:, k, n_ * 512:(n_ + 1) * 512],
                        start=(k == 0), stop=(k == KCH - 1)),
                        r=[wkey, "xn"], w=[bkey])
                if m < 3:
                    p.add("vector", lambda e, n_=n_, bank=bank, mx=mx: e.tensor_copy(
                        mx[:, 3 + n_ * 512: 3 + (n_ + 1) * 512], ps[bank][:, 0:512]),
                        r=[bkey], w=[mxkey])
                else:
                    p.add("scalar", lambda e, n_=n_, bank=bank: e.activation(
                        zst[:, n_ * 512:(n_ + 1) * 512], ps[bank][:, 0:512], AF.Silu),
                        r=[bkey], w=["zst"])
            if m == 3:
                p.add("sync", lambda e: e.dma_start_transpose(
                    ztm[:, hh, :, :], zst[:]),
                    r=["zst"], w=["ztm"], dma=True)
                continue
            gi = hh * 3 + m
            p.add("vector", lambda e, mx=mx, gi=gi: e.tensor_copy(
                mx[:, 0:3], tails[:, gi, :]),
                r=["tails"], w=[mxkey])
            p.add("vector", lambda e, mx=mx, gi=gi: e.tensor_scalar(
                cvac[:], mx[:, 0:1024], convw[:, gi, 0:1], None, ALU.mult),
                r=[mxkey, "convw"], w=["cvac"])
            for j in range(1, 4):
                p.add("vector", lambda e, mx=mx, gi=gi, j=j: e.scalar_tensor_tensor(
                    cvac[:], mx[:, j:j + 1024], convw[:, gi, j:j + 1], cvac[:],
                    ALU.mult, ALU.add),
                    r=[mxkey, "convw", "cvac"], w=["cvac"])
            p.add("vector", lambda e, mx=mx, gi=gi: e.tensor_copy(
                tails[:, gi, :], mx[:, 1024:1027]),
                r=[mxkey], w=["tails"])
            p.add("scalar", lambda e, m=m: e.activation(
                outb[m][:], cvac[:], AF.Silu),
                r=["cvac"], w=[okey[m]])
        if "qkv" in dbg_d and ps_i == 0 and hh == 0:
            for mi_, (buf_, bkey_) in enumerate([(qf, "qf"), (kf, "kf"),
                                                 (vf, "vf"), (zst, "zst")]):
                p.add("sync", lambda e, mi_=mi_, buf_=buf_: e.dma_start(
                    out=dbg_d["qkv"][mi_], in_=buf_[:]), r=[bkey_], dma=True)
        p.add("sync", lambda e: e.dma_start_transpose(qtm[:], qf[:]),
              r=["qf"], w=["qtm"], dma=True)
        p.add("sync", lambda e: e.dma_start_transpose(ktm[:], kf[:]),
              r=["kf"], w=["ktm"], dma=True)
        p.add("sync", lambda e: e.dma_start_transpose(vtm[:], vf[:]),
              r=["vf"], w=["vtm"], dma=True)
        if "ktm" in dbg_d and ps_i == 0 and hh == 0:
            dbg_store("ktm", ktm[:].rearrange("p a b -> p (a b)"), ["ktm"])
        p.add("vector", lambda e, hh=hh: e.tensor_copy(
            hcols[:, 0, :], scpack[:, :, 24 + hh]),
            r=["scpack"], w=["hcols"])
        p.add("vector", lambda e, hh=hh: e.tensor_copy(
            hcols[:, 1, :], scpack[:, :, hh]),
            r=["scpack"], w=["hcols"])
        p.add("gpsimd", lambda e: e.tensor_mul(
            sq2[:].rearrange("p a b -> p (a b)"),
            ktm[:].rearrange("p a b -> p (a b)"),
            ktm[:].rearrange("p a b -> p (a b)")),
            r=["ktm"], w=["sq2"])
        p.add("vector", lambda e: e.reduce_sum(
            hcols[:, 2, :].rearrange("p (a b) -> p a b", b=1), sq2[:], axis=AX.X),
            r=["sq2"], w=["hcols"])
        p.add("gpsimd", lambda e: e.tensor_mul(
            sq2[:].rearrange("p a b -> p (a b)"),
            qtm[:].rearrange("p a b -> p (a b)"),
            qtm[:].rearrange("p a b -> p (a b)")),
            r=["qtm"], w=["sq2"])
        p.add("vector", lambda e: e.reduce_sum(
            hcols[:, 3, :].rearrange("p (a b) -> p a b", b=1), sq2[:], axis=AX.X),
            r=["sq2"], w=["hcols"])
        p.add("scalar", lambda e: e.activation(
            hcols[:, 2, :], hcols[:, 2, :], AF.Ln, bias=cepsc[:, 1:2]),
            r=["hcols", "c_epsc"], w=["hcols"])
        p.add("scalar", lambda e: e.activation(
            hcols[:, 3, :], hcols[:, 3, :], AF.Ln, bias=cepsc[:, 1:2]),
            r=["hcols", "c_epsc"], w=["hcols"])
        p.add("vector", lambda e: e.tensor_sub(
            hcols[:, 4, :], hcols[:, 0, :], hcols[:, 1, :]),
            r=["hcols"], w=["hcols"])
        p.add("vector", lambda e: e.scalar_tensor_tensor(
            hcols[:, 4, :], hcols[:, 2, :], -0.5, hcols[:, 4, :],
            ALU.mult, ALU.add),
            r=["hcols"], w=["hcols"])
        p.add("vector", lambda e: e.scalar_tensor_tensor(
            hcols[:, 5, :], hcols[:, 2, :], 0.5, hcols[:, 0, :],
            ALU.mult, ALU.add),
            r=["hcols"], w=["hcols"])
        p.add("vector", lambda e: e.scalar_tensor_tensor(
            hcols[:, 6, :], hcols[:, 3, :], -0.5, hcols[:, 0, :],
            ALU.mult, ALU.add),
            r=["hcols"], w=["hcols"])
        p.add("vector", lambda e: e.tensor_scalar_add(
            hcols[:, 6, :], hcols[:, 6, :], LNDK),
            r=["hcols"], w=["hcols"])
        p.add("scalar", lambda e: e.activation(
            hcols[:, 7, :], hcols[:, 1, :], AF.Exp, scale=-1.0),
            r=["hcols"], w=["hcols"])
        p.add("scalar", lambda e: e.activation(
            hcols[:, 8, :], hcols[:, 4, :], AF.Exp),
            r=["hcols"], w=["hcols"])
        p.add("vector", lambda e: e.tensor_scalar_mul(
            hcols[:, 8, :], hcols[:, 8, :], -1.0),
            r=["hcols"], w=["hcols"])
        p.add("scalar", lambda e: e.activation(
            hcols[:, 12, :], hcols[:, 6, :], AF.Exp),
            r=["hcols"], w=["hcols"])
        p.add("tensor", lambda e: e.matmul(
            ps[1][:, 384:384 + CPP], ce127[:], hcols[:, 0, :],
            start=True, stop=True),
            r=["c_e127", "hcols"], w=["ps1"])
        p.add("vector", lambda e: e.tensor_copy(
            hcols[:, 10, :], ps[1][:, 384:384 + CPP]),
            r=["ps1"], w=["hcols"])
        p.add("scalar", lambda e: e.activation(
            hcols[:, 11, :], hcols[:, 10, :], AF.Exp),
            r=["hcols"], w=["hcols"])
        p.add("vector", lambda e: e.tensor_sub(
            hcols[:, 9, :], hcols[:, 10, :], hcols[:, 5, :]),
            r=["hcols"], w=["hcols"])
        p.add("scalar", lambda e: e.activation(
            hcols[:, 9, :], hcols[:, 9, :], AF.Exp),
            r=["hcols"], w=["hcols"])
        import itertools
        for c0 in range(0, CPP, 2):
            ga = pre_steps(ps_i, hh, c0)
            gb = pre_steps(ps_i, hh, c0 + 1)
            for sa, sb in itertools.zip_longest(ga, gb, fillvalue=[]):
                for op in list(sa) + list(sb):
                    eng, emit, r_, w_ = op[0], op[1], op[2], op[3]
                    dma = op[4] if len(op) > 4 else False
                    p.add(eng, emit, r=r_, w=w_, dma=dma)
        for ck in range(CPP):
            emit_chain(ps_i, hh, ck)
        p.add("sync", lambda e: e.dma_start_transpose(ofm[:], ogtm[:].rearrange("p a b -> p (a b)")),
              r=["ogtm"], w=["ofm"], dma=True)
        p.add("sync", lambda e, hh=hh, ps_i=ps_i: e.dma_start(
            out=cc_in[ps_i, hh],
            in_=ofm[:].rearrange("p a b -> p (a b)")),
            r=["ofm"], w=["cc_in"], dma=True)

    def emit_pass(ps_i):
        t0 = ps_i * PASS_T
        for st in range(8):
            xb = xld[st % 2]
            key = f"xld{st % 2}"
            rows = slice(t0 + st * 128, t0 + st * 128 + 128)
            p.add("sync", lambda e, rows=rows, xb=xb: e.dma_start(
                out=xb[:], in_=x_d[rows, :]), w=[key], dma=True)
            for j in range(4):
                p.add("vector", lambda e, xb=xb, j=j: e.bn_stats(
                    bst[:, j, :], xb[:, j * 512:(j + 1) * 512]),
                    r=[key], w=["bst"], special=True)
            p.add("vector", lambda e: e.bn_aggr(
                lnc_[:, 0:2], bst[:].rearrange("p a b -> p (a b)")),
                r=["bst"], w=["lnc"], special=True)
            p.add("scalar", lambda e: e.activation(
                lnc_[:, 1:2], lnc_[:, 1:2], AF.Ln, bias=cepsc[:, 0:1]),
                r=["lnc", "c_epsc"], w=["lnc"])
            p.add("scalar", lambda e: e.activation(
                lnc_[:, 1:2], lnc_[:, 1:2], AF.Exp, scale=-0.5),
                r=["lnc"], w=["lnc"])
            p.add("vector", lambda e, xb=xb: e.tensor_scalar(
                xtm[:], xb[:], lnc_[:, 0:1], lnc_[:, 1:2],
                ALU.subtract, ALU.mult),
                r=[key, "lnc"], w=["xtm"])
            p.add("sync", lambda e, st=st: e.dma_start_transpose(
                xn[:, :, st * 128:(st + 1) * 128], xtm[:]),
                r=["xtm"], w=["xn"], dma=True)
        if "xn" in dbg_d and ps_i == 0:
            dbg_store("xn", xn[:].rearrange("p a b -> p (a b)"), ["xn"])

        p.add("gpsimd", lambda e: e.dma_start(
            out=wbuf[0][:, :, 0:16],
            in_=wab_d.rearrange("(k q) c -> q k c", q=128)),
            w=["wbuf0"], dma=True)
        for n_ in range(2):
            for k in range(KCH):
                p.add("tensor", lambda e, n_=n_, k=k: e.matmul(
                    ps[0][0:16, 0:512], wbuf[0][:, k, 0:16],
                    xn[:, k, n_ * 512:(n_ + 1) * 512],
                    start=(k == 0), stop=(k == KCH - 1)),
                    r=["wbuf0", "xn"], w=["ps0"])
            p.add("vector", lambda e, n_=n_: e.tensor_copy(
                scf[:, 0, n_ * 512:(n_ + 1) * 512], ps[0][0:16, 0:512]),
                r=["ps0"], w=["scf"])
        p.add("vector", lambda e: e.tensor_scalar(
            scf[:, 1, :], scf[:, 0, :], csgn[:, 0:1], cbia[:, 0:1],
            ALU.mult, ALU.add),
            r=["scf", "c_sgn", "c_bia"], w=["scf"])
        p.add("scalar", lambda e: e.activation(scf[:, 0, :], scf[:, 1, :], AF.Abs),
              r=["scf"], w=["scf"])
        p.add("scalar", lambda e: e.activation(scf[:, 0, :], scf[:, 0, :],
                                               AF.Exp, scale=-1.0),
              r=["scf"], w=["scf"])
        p.add("vector", lambda e: e.tensor_scalar_add(scf[:, 0, :], scf[:, 0, :],
                                                      1.0),
              r=["scf"], w=["scf"])
        p.add("scalar", lambda e: e.activation(scf[:, 0, :], scf[:, 0, :], AF.Ln),
              r=["scf"], w=["scf"])
        p.add("vector", lambda e: e.tensor_relu(scf[:, 1, :], scf[:, 1, :]),
              r=["scf"], w=["scf"])
        p.add("vector", lambda e: e.tensor_add(scf[:, 0, :], scf[:, 0, :],
                                               scf[:, 1, :]),
              r=["scf"], w=["scf"])
        p.add("vector", lambda e: e.tensor_scalar(
            scG[:], scf[:, 0, :], negal[:, 0:1], None, ALU.mult),
            r=["scf", "negal"], w=["scG"])
        for ck in range(CPP):
            p.add("vector", lambda e, ck=ck: e.tensor_tensor_scan(
                scB[:, ck * C:(ck + 1) * C], cones[0:16, 0:128],
                scG[:, ck * C:(ck + 1) * C], 0.0, ALU.mult, ALU.add),
                r=["scG", "c_ones"], w=["scB"], special=True)
        if "sc" in dbg_d and ps_i == 0:
            dbg_store("sc", scG[:], ["scG"])
        if "scB" in dbg_d and ps_i == 0:
            dbg_store("scB", scB[:], ["scB"])
        for ck in range(CPP):
            p.add("tensor", lambda e, ck=ck: e.transpose(
                ps[1][:, 0:16], scG[:, ck * C:(ck + 1) * C], cidf[0:16, 0:16]),
                r=["scG", "c_idf"], w=["ps1"])
            p.add("vector", lambda e, ck=ck: e.tensor_copy(
                scpack[:, ck, 0:16], ps[1][:, 0:16]),
                r=["ps1"], w=["scpack"])
            p.add("tensor", lambda e, ck=ck: e.transpose(
                ps[1][:, 16:32], scB[:, ck * C:(ck + 1) * C], cidf[0:16, 0:16]),
                r=["scB", "c_idf"], w=["ps1"])
            p.add("vector", lambda e, ck=ck: e.tensor_copy(
                scpack[:, ck, 16:32], ps[1][:, 16:32]),
                r=["ps1"], w=["scpack"])
        for hh in range(HG):
            emit_head(ps_i, hh)

    for ps_i in range(NPASS):
        emit_pass(ps_i)

    p.add("sync", lambda e: e.dma_start(out=gus[:], in_=gus_d),
          w=["c_gus"], dma=True)
    p.add("sync", lambda e: e.dma_start(out=dns[:], in_=dns_d),
          w=["c_dns"], dma=True)
    p.add("gpsimd", lambda e: e.collective_compute(
        "AllGather", ALU.bypass, RG, [cc_in], [cc_out]),
        r=["cc_in"], w=["cc_out"], special=True)
    _pid = {}

    def _ofm_load(e, r_, h_):
        if "th" not in _pid:
            _pid["th"] = e.partition_id() % 2
        th = _pid["th"]
        src5 = cc_out[r_, :, h_, :, :]
        return e.dma_start(
            out=ofm_t[:, r_ * HG + h_, :],
            in_=src5[bass.ts(th, 1)].rearrange("a p t -> p (a t)"))
    for r_ in range(2):
        for h_ in range(HG):
            p.add("sync", lambda e, r_=r_, h_=h_: _ofm_load(e, r_, h_),
                  r=["cc_out"], w=["ofm_t"], dma=True)

    def ln_stats(srckey, src_ap):
        for j in range(4):
            p.add("vector", lambda e, j=j, src_ap=src_ap: e.bn_stats(
                bst[:, j, :], src_ap[:, j * 512:(j + 1) * 512]),
                r=[srckey], w=["bst"], special=True)
        p.add("vector", lambda e: e.bn_aggr(
            lnc_[:, 0:2], bst[:].rearrange("p a b -> p (a b)")),
            r=["bst"], w=["lnc"], special=True)
        p.add("scalar", lambda e: e.activation(
            lnc_[:, 1:2], lnc_[:, 1:2], AF.Ln, bias=cepsc[:, 0:1]),
            r=["lnc", "c_epsc"], w=["lnc"])
        p.add("scalar", lambda e: e.activation(
            lnc_[:, 1:2], lnc_[:, 1:2], AF.Exp, scale=-0.5),
            r=["lnc"], w=["lnc"])

    def ln_residual(src, srckey, make_xn1):
        for st in range(8):
            rows = slice(st * 128, st * 128 + 128)
            res_dram = xh_d if make_xn1 else x1s_d
            xb = xld[st % 2]
            xkey = f"xld{st % 2}"
            p.add("sync", lambda e, rows=rows, xb=xb, res_dram=res_dram:
                  e.dma_start(out=xb[:], in_=res_dram[rows, :]),
                  w=[xkey], dma=True)
            for kc in range(KCH):
                p.add("sync", lambda e, st=st, kc=kc, src=src: e.dma_start_transpose(
                    atm[:, kc * 128:(kc + 1) * 128],
                    src[:, kc, st * 128:(st + 1) * 128]),
                    r=[srckey], w=["atm"], dma=True)
            ln_stats("atm", atm[:])
            p.add("vector", lambda e: e.tensor_scalar(
                x1t[:], atm[:], lnc_[:, 0:1], lnc_[:, 1:2],
                ALU.subtract, ALU.mult),
                r=["atm", "lnc"], w=["x1t"])
            p.add("vector", lambda e, xb=xb: e.tensor_add(x1t[:], x1t[:], xb[:]),
                  r=["x1t", xkey], w=["x1t"])
            if make_xn1:
                p.add("sync", lambda e, rows=rows: e.dma_start(
                    out=x1s_d[rows, :], in_=x1t[:]), r=["x1t"], dma=True)
                if "x1" in dbg_d and st == 0:
                    p.add("sync", lambda e: e.dma_start(
                        out=dbg_d["x1"], in_=x1t[:]), r=["x1t"], dma=True)
                ln_stats("x1t", x1t[:])
                p.add("vector", lambda e: e.tensor_scalar(
                    x1n[:], x1t[:], lnc_[:, 0:1], lnc_[:, 1:2],
                    ALU.subtract, ALU.mult),
                    r=["x1t", "lnc"], w=["x1n"])
                p.add("sync", lambda e, st=st: e.dma_start_transpose(
                    xn[:, :, st * 128:(st + 1) * 128], x1n[:]),
                    r=["x1n"], w=["xn"], dma=True)
            else:
                p.add("sync", lambda e, rows=rows: e.dma_start(
                    out=out_d[rows, :], in_=x1t[:]), r=["x1t"], dma=True)

    for mg in range(4):
        wb = wbuf[mg % 2]
        wkey = f"wbuf{mg % 2}"
        p.add("gpsimd", lambda e, mg=mg, wb=wb: e.dma_start(
            out=wb[:],
            in_=wout_d.rearrange("(k q) c -> q k c", q=128)[:, :, mg * 512:(mg + 1) * 512]),
            w=[wkey], dma=True)
        for mt in range(4):
            m = mg * 4 + mt
            for n_ in range(2):
                bank = n_
                for k in range(KCH):
                    p.add("tensor", lambda e, wb=wb, mt=mt, n_=n_, k=k, bank=bank: e.matmul(
                        ps[bank][:, 0:512], wb[:, k, mt * 128:(mt + 1) * 128],
                        ofm_t[:, k, n_ * 512:(n_ + 1) * 512],
                        start=(k == 0), stop=(k == KCH - 1)),
                        r=[wkey, "ofm_t"], w=[f"ps{bank}"])
                p.add("vector", lambda e, m=m, n_=n_, bank=bank: e.tensor_copy(
                    afm[:, m, n_ * 512:(n_ + 1) * 512], ps[bank][:, 0:512]),
                    r=[f"ps{bank}"], w=["afm"])
    if "attn" in dbg_d:
        p.add("sync", lambda e: e.dma_start(
            out=dbg_d["attn"], in_=afm[:].rearrange("q k t -> q (k t)")),
            r=["afm"], dma=True)

    ln_residual(afm, "afm", True)

    for th2 in range(2):
        tc = slice(th2 * 512, th2 * 512 + 512)
        for mg in range(22):
            wb = wbuf[mg % 2]
            wkey = f"wbuf{mg % 2}"
            p.add("gpsimd", lambda e, mg=mg, wb=wb: e.dma_start(
                out=wb[:, :, 0:256],
                in_=wgu_d.rearrange("(k q) c -> q k c", q=128)[:, :, mg * 256:(mg + 1) * 256]),
                w=[wkey], dma=True)
            p.add("gpsimd", lambda e, mg=mg, wb=wb: e.dma_start(
                out=wb[:, :, 256:512],
                in_=wgu_d.rearrange("(k q) c -> q k c", q=128)[:, :, INTER + mg * 256: INTER + (mg + 1) * 256]),
                w=[wkey], dma=True)
            for mt in range(2):
                m = mg * 2 + mt
                for k in range(KCH):
                    p.add("tensor", lambda e, wb=wb, mt=mt, k=k, tc=tc: e.matmul(
                        ps[0][:, 0:512], wb[:, k, mt * 128:(mt + 1) * 128],
                        xn[:, k, tc], start=(k == 0), stop=(k == KCH - 1)),
                        r=[wkey, "xn"], w=["ps0"])
                for k in range(KCH):
                    p.add("tensor", lambda e, wb=wb, mt=mt, k=k, tc=tc: e.matmul(
                        ps[1][:, 0:512], wb[:, k, 256 + mt * 128: 256 + (mt + 1) * 128],
                        xn[:, k, tc], start=(k == 0), stop=(k == KCH - 1)),
                        r=[wkey, "xn"], w=["ps1"])
                p.add("scalar", lambda e, m=m: e.activation(
                    sqs[:, 0:512], ps[0][:, 0:512], AF.Silu,
                    scale=gus[:, m:m + 1]),
                    r=["ps0", "c_gus"], w=["sqs"])
                p.add("vector", lambda e, m=m: e.scalar_tensor_tensor(
                    hfm[:, m, :], ps[1][:, 0:512], gus[:, MI + m: MI + m + 1],
                    sqs[:, 0:512], ALU.mult, ALU.mult),
                    r=["ps1", "c_gus", "sqs"], w=["hfm"])
        if "hfm" in dbg_d and th2 == 0:
            p.add("sync", lambda e: e.dma_start(
                out=dbg_d["hfm"][:, 0:512], in_=hfm[:, 0, :]), r=["hfm"], dma=True)
        for mg in range(4):
            for kg in range(4):
                wb = wbuf[kg % 2]
                wkey = f"wbuf{kg % 2}"
                p.add("gpsimd", lambda e, mg=mg, kg=kg, wb=wb: e.dma_start(
                    out=wb[:, 0:11, :],
                    in_=wdn_d.rearrange("(k q) c -> q k c", q=128)[:, kg * 11:(kg + 1) * 11, mg * 512:(mg + 1) * 512]),
                    w=[wkey], dma=True)
                for mt in range(4):
                    bank = 4 + mt
                    for kk in range(11):
                        kglob = kg * 11 + kk
                        p.add("tensor", lambda e, wb=wb, mt=mt, kk=kk, kglob=kglob, bank=bank, kg=kg: e.matmul(
                            ps[bank][:, 0:512], wb[:, kk, mt * 128:(mt + 1) * 128],
                            hfm[:, kglob, :],
                            start=(kg == 0 and kk == 0),
                            stop=(kg == 3 and kk == 10)),
                            r=[wkey, "hfm"], w=[f"ps{bank}"])
            for mt in range(4):
                m = mg * 4 + mt
                bank = 4 + mt
                p.add("vector", lambda e, m=m, bank=bank, tc=tc: e.tensor_scalar(
                    ofm_t[:, m, tc], ps[bank][:, 0:512], dns[:, m:m + 1], None,
                    ALU.mult),
                    r=[f"ps{bank}", "c_dns"], w=["ofm_t"])

    ln_residual(ofm_t, "ofm_t", False)

    with nc.Block() as block:
        p.finalize(block)
    return nc, sorted(dbg_d)




def build_tail(dbg=()):
    nc = bass.Bass("TRN2", target_bir_lowering=False, debug=False,
                   detect_race_conditions=False)
    p = Prog(nc)
    TH = 1024

    o_d = nc.dram_tensor("o", [H, 128, TH], BF16, kind="ExternalInput").ap()
    xh_d = nc.dram_tensor("xh", [TH, D], F32, kind="ExternalInput").ap()
    wout_d = nc.dram_tensor("wout", [D, D], F32, kind="ExternalInput").ap()
    wgu_d = nc.dram_tensor("wgu", [D, 2 * INTER], F32, kind="ExternalInput").ap()
    wdn_d = nc.dram_tensor("wdn", [INTER, D], F32, kind="ExternalInput").ap()
    gus_d = nc.dram_tensor("gus", [128, 2 * MI], F32, kind="ExternalInput").ap()
    dns_d = nc.dram_tensor("dns", [128, KCH], F32, kind="ExternalInput").ap()
    epsc_d = nc.dram_tensor("epsc", [128, 4], F32, kind="ExternalInput").ap()
    out_d = nc.dram_tensor("out", [TH, D], F32, kind="ExternalOutput").ap()
    x1s_d = nc.dram_tensor("x1spill", [TH, D], F32, kind="Internal").ap()

    dbg_d = {}
    for nm, shp, dt in [("attn", [128, KCH * TH], BF16),
                        ("x1", [128, D], F32),
                        ("hfm", [128, TH], BF16)]:
        if nm in dbg:
            dbg_d[nm] = nc.dram_tensor("dbg_" + nm, shp, dt,
                                       kind="ExternalOutput").ap()

    Al = nc.alloc_sbuf_tensor
    ofm = Al("ofm", [128, H, TH], BF16)
    wbuf = Al("wbuf", [128, KCH, 512], BF16)
    afm = Al("afm", [128, KCH, TH], BF16)
    atm = Al("atm", [128, D], BF16)
    xld = Al("xld", [128, D], F32)
    x1t = Al("x1t", [128, D], F32)
    x1n = Al("x1n", [128, D], BF16)
    xn1 = Al("xn1", [128, KCH, TH], BF16)
    hfm = Al("hfm", [128, MI, 512], BF16)
    lnc_ = Al("lncols", [128, 4], F32)
    bst = Al("bnst", [128, 4, 6], F32)
    sqs = Al("sqs", [128, 512], F32)
    gus = Al("c_gus", [128, 2 * MI], F32)
    dns = Al("c_dns", [128, KCH], F32)
    cepsc = Al("c_epsc", [128, 4], F32)
    ps = [nc.alloc_psum_tensor(f"ps{i}", [128, 512], F32) for i in range(8)]

    def ln_stats(srckey, src_ap):
        for j in range(4):
            p.add("vector", lambda e, j=j: e.bn_stats(
                bst[:, j, :], src_ap[:, j * 512:(j + 1) * 512]),
                r=[srckey], w=["bst"], special=True)
        p.add("vector", lambda e: e.bn_aggr(
            lnc_[:, 0:2], bst[:].rearrange("p a b -> p (a b)")),
            r=["bst"], w=["lnc"], special=True)
        p.add("scalar", lambda e: e.activation(
            lnc_[:, 1:2], lnc_[:, 1:2], AF.Ln, bias=cepsc[:, 0:1]),
            r=["lnc", "c_epsc"], w=["lnc"])
        p.add("scalar", lambda e: e.activation(
            lnc_[:, 1:2], lnc_[:, 1:2], AF.Exp, scale=-0.5), r=["lnc"], w=["lnc"])

    def ln_residual(srckey, make_xn1):
        for st in range(8):
            rows = slice(st * 128, st * 128 + 128)
            res_dram = xh_d if make_xn1 else x1s_d
            p.add("sync", lambda e, rows=rows, res_dram=res_dram: e.dma_start(
                out=xld[:], in_=res_dram[rows, :]), w=["xld"], dma=True)
            for kc in range(KCH):
                p.add("sync", lambda e, st=st, kc=kc: e.dma_start_transpose(
                    atm[:, kc * 128:(kc + 1) * 128],
                    afm[:, kc, st * 128:(st + 1) * 128]),
                    r=[srckey], w=["atm"], dma=True)
            ln_stats("atm", atm[:])
            p.add("vector", lambda e: e.tensor_scalar(
                x1t[:], atm[:], lnc_[:, 0:1], lnc_[:, 1:2],
                ALU.subtract, ALU.mult),
                r=["atm", "lnc"], w=["x1t"])
            p.add("vector", lambda e: e.tensor_add(x1t[:], x1t[:], xld[:]),
                  r=["x1t", "xld"], w=["x1t"])
            if make_xn1:
                p.add("sync", lambda e, rows=rows: e.dma_start(
                    out=x1s_d[rows, :], in_=x1t[:]), r=["x1t"], dma=True)
                if "x1" in dbg_d and st == 0:
                    p.add("sync", lambda e: e.dma_start(
                        out=dbg_d["x1"], in_=x1t[:]), r=["x1t"], dma=True)
                ln_stats("x1t", x1t[:])
                p.add("vector", lambda e: e.tensor_scalar(
                    x1n[:], x1t[:], lnc_[:, 0:1], lnc_[:, 1:2],
                    ALU.subtract, ALU.mult),
                    r=["x1t", "lnc"], w=["x1n"])
                p.add("sync", lambda e, st=st: e.dma_start_transpose(
                    xn1[:, :, st * 128:(st + 1) * 128], x1n[:]),
                    r=["x1n"], w=["xn1"], dma=True)
            else:
                p.add("sync", lambda e, rows=rows: e.dma_start(
                    out=out_d[rows, :], in_=x1t[:]), r=["x1t"], dma=True)

    p.add("sync", lambda e: e.dma_start(
        out=ofm[:], in_=o_d.rearrange("h q t -> q h t")), w=["ofm"], dma=True)
    p.add("sync", lambda e: e.dma_start(out=gus[:], in_=gus_d), w=["gus"], dma=True)
    p.add("sync", lambda e: e.dma_start(out=dns[:], in_=dns_d), w=["dns"], dma=True)
    p.add("sync", lambda e: e.dma_start(out=cepsc[:], in_=epsc_d), w=["c_epsc"], dma=True)
    p.add("vector", lambda e: e.memset(bst[:], 0.0), w=["bst"])
    p.add("vector", lambda e: e.bn_stats(bst[:, 0, :], bst[:, 2, 0:6]),
          r=["bst"], w=["bst", "warm"])
    p.add("vector", lambda e: e.bn_aggr(bst[:, 1, 0:2], bst[:, 0, :]),
          r=["bst"], w=["bst", "warm"])

    for mg in range(4):
        p.add("gpsimd", lambda e, mg=mg: e.dma_start(
            out=wbuf[:],
            in_=wout_d.rearrange("(k q) c -> q k c", q=128)[:, :, mg * 512:(mg + 1) * 512]),
            w=["wbuf"], dma=True)
        for mt in range(4):
            m = mg * 4 + mt
            for n_ in range(2):
                bank = n_
                for k in range(KCH):
                    p.add("tensor", lambda e, mt=mt, n_=n_, k=k, bank=bank: e.matmul(
                        ps[bank][:, 0:512], wbuf[:, k, mt * 128:(mt + 1) * 128],
                        ofm[:, k, n_ * 512:(n_ + 1) * 512],
                        start=(k == 0), stop=(k == KCH - 1)),
                        r=["wbuf", "ofm"], w=[f"ps{bank}"])
                p.add("vector", lambda e, m=m, n_=n_, bank=bank: e.tensor_copy(
                    afm[:, m, n_ * 512:(n_ + 1) * 512], ps[bank][:, 0:512]),
                    r=[f"ps{bank}"], w=["afm"])
    if "attn" in dbg_d:
        p.add("sync", lambda e: e.dma_start(
            out=dbg_d["attn"], in_=afm[:].rearrange("q k t -> q (k t)")),
            r=["afm"], dma=True)

    ln_residual("afm", True)

    for th2 in range(2):
        tc = slice(th2 * 512, th2 * 512 + 512)
        for mg in range(22):
            p.add("gpsimd", lambda e, mg=mg: e.dma_start(
                out=wbuf[:, :, 0:256],
                in_=wgu_d.rearrange("(k q) c -> q k c", q=128)[:, :, mg * 256:(mg + 1) * 256]),
                w=["wbuf"], dma=True)
            p.add("gpsimd", lambda e, mg=mg: e.dma_start(
                out=wbuf[:, :, 256:512],
                in_=wgu_d.rearrange("(k q) c -> q k c", q=128)[:, :, INTER + mg * 256: INTER + (mg + 1) * 256]),
                w=["wbuf"], dma=True)
            for mt in range(2):
                m = mg * 2 + mt
                for k in range(KCH):
                    p.add("tensor", lambda e, mt=mt, k=k, tc=tc: e.matmul(
                        ps[0][:, 0:512], wbuf[:, k, mt * 128:(mt + 1) * 128],
                        xn1[:, k, tc], start=(k == 0), stop=(k == KCH - 1)),
                        r=["wbuf", "xn1"], w=["ps0"])
                for k in range(KCH):
                    p.add("tensor", lambda e, mt=mt, k=k, tc=tc: e.matmul(
                        ps[1][:, 0:512], wbuf[:, k, 256 + mt * 128: 256 + (mt + 1) * 128],
                        xn1[:, k, tc], start=(k == 0), stop=(k == KCH - 1)),
                        r=["wbuf", "xn1"], w=["ps1"])
                p.add("scalar", lambda e, m=m: e.activation(
                    sqs[:, 0:512], ps[0][:, 0:512], AF.Silu,
                    scale=gus[:, m:m + 1]),
                    r=["ps0", "gus"], w=["sqs"])
                p.add("vector", lambda e, m=m: e.scalar_tensor_tensor(
                    hfm[:, m, :], ps[1][:, 0:512], gus[:, MI + m: MI + m + 1],
                    sqs[:, 0:512], ALU.mult, ALU.mult),
                    r=["ps1", "gus", "sqs"], w=["hfm"])
        if "hfm" in dbg_d and th2 == 0:
            p.add("sync", lambda e: e.dma_start(
                out=dbg_d["hfm"][:, 0:512], in_=hfm[:, 0, :]), r=["hfm"], dma=True)
        for mg in range(4):
            for kg in range(4):
                p.add("gpsimd", lambda e, mg=mg, kg=kg: e.dma_start(
                    out=wbuf[:, 0:11, :],
                    in_=wdn_d.rearrange("(k q) c -> q k c", q=128)[:, kg * 11:(kg + 1) * 11, mg * 512:(mg + 1) * 512]),
                    w=["wbuf"], dma=True)
                for mt in range(4):
                    bank = 4 + mt
                    for kk in range(11):
                        kglob = kg * 11 + kk
                        p.add("tensor", lambda e, mt=mt, kk=kk, kglob=kglob, bank=bank, kg=kg: e.matmul(
                            ps[bank][:, 0:512], wbuf[:, kk, mt * 128:(mt + 1) * 128],
                            hfm[:, kglob, :],
                            start=(kg == 0 and kk == 0),
                            stop=(kg == 3 and kk == 10)),
                            r=["wbuf", "hfm"], w=[f"ps{bank}"])
            for mt in range(4):
                m = mg * 4 + mt
                bank = 4 + mt
                p.add("vector", lambda e, m=m, bank=bank, tc=tc: e.tensor_scalar(
                    afm[:, m, tc], ps[bank][:, 0:512], dns[:, m:m + 1], None,
                    ALU.mult),
                    r=[f"ps{bank}", "dns"], w=["afm"])

    ln_residual("afm", False)

    with nc.Block() as block:
        p.finalize(block)
    return nc, sorted(dbg_d)


def _consts():
    identf = np.eye(128, dtype=np.float32)
    ibf = np.eye(128, dtype=BF)
    onesf = np.ones((128, 128), np.float32)
    e127 = np.zeros((128, 128), np.float32)
    e127[127, :] = 1.0
    ii = np.arange(128)
    m_strictlow = (ii[:, None] > ii[None, :]).astype(np.float32)
    m_strictup = (ii[:, None] < ii[None, :]).astype(np.float32)
    m_upincl = (ii[:, None] <= ii[None, :]).astype(np.float32)
    maskm = np.ascontiguousarray(
        np.concatenate([m_strictlow, m_strictup, m_upincl], axis=1))
    maska = np.ascontiguousarray((1.0 - maskm) * FILL)
    epsc = np.zeros((128, 4), np.float32)
    epsc[:, 0] = LNEPS
    epsc[:, 1] = EPS6
    epsc[:, 2] = 1.0
    return identf, ibf, onesf, e127, maskm, maska, epsc


def kernel(**inputs):
    xfull = np.ascontiguousarray(np.asarray(inputs["L0_attn_hidden_states"],
                                            np.float32))
    Wqkv = np.asarray(inputs["L0_attn_W_qkv"], np.float32)
    convw = np.asarray(inputs["L0_attn_conv_w"], np.float32)
    Wz = np.asarray(inputs["L0_attn_W_z"], np.float32)
    Wb = np.asarray(inputs["L0_attn_W_b"], np.float32)
    Wa = np.asarray(inputs["L0_attn_W_a"], np.float32)
    dtb = np.asarray(inputs["L0_attn_dt_bias"], np.float32)
    Alog = np.asarray(inputs["L0_attn_A_log"], np.float32)
    nw = np.asarray(inputs["L0_attn_norm_weight"], np.float32)
    Wout = np.ascontiguousarray(np.asarray(inputs["L0_attn_W_out"], np.float32))
    gw = np.asarray(inputs["L0_ffn_gate_w"], np.float32)
    uw = np.asarray(inputs["L0_ffn_up_w"], np.float32)
    dw = np.asarray(inputs["L0_ffn_down_w"], np.float32)
    gs = np.asarray(inputs["L0_ffn_gate_s"], np.float32)
    us = np.asarray(inputs["L0_ffn_up_s"], np.float32)
    ds = np.asarray(inputs["L0_ffn_down_s"], np.float32)

    identf, ibf, onesf, e127, maskm, maska, epsc = _consts()
    nwt = np.ascontiguousarray(
        np.broadcast_to(nw[None, :], (128, 128)).astype(np.float32))
    wgu = np.ascontiguousarray(
        np.concatenate([gw.T, uw.T], axis=1).astype(np.float32))
    wdnT = np.ascontiguousarray(dw.T)
    gus = np.ascontiguousarray(np.concatenate(
        [gs.reshape(MI, 128).T, us.reshape(MI, 128).T], axis=1))
    dns = np.ascontiguousarray(ds.reshape(KCH, 128).T)

    in_maps = []
    for c in range(8):
        b, hg = c // 2, c % 2
        whead = np.empty((D, HG * 512), np.float32)
        conv_flat = np.empty((HG * 3 * 128, 4), np.float32)
        for h in range(HG):
            gh = hg * HG + h
            whead[:, h * 512 + 0:h * 512 + 128] = Wqkv[:, gh * 128:(gh + 1) * 128]
            whead[:, h * 512 + 128:h * 512 + 256] = Wqkv[:, 2048 + gh * 128: 2048 + (gh + 1) * 128]
            whead[:, h * 512 + 256:h * 512 + 384] = Wqkv[:, 4096 + gh * 128: 4096 + (gh + 1) * 128]
            whead[:, h * 512 + 384:h * 512 + 512] = Wz[:, gh * 128:(gh + 1) * 128]
            for m, base in enumerate((0, 2048, 4096)):
                conv_flat[(h * 3 + m) * 128:(h * 3 + m + 1) * 128, :] = \
                    convw[base + gh * 128: base + (gh + 1) * 128, 0, :]
        wab = np.ascontiguousarray(np.concatenate(
            [Wb[:, hg * 8:hg * 8 + 8], Wa[:, hg * 8:hg * 8 + 8]], axis=1))
        spsign = np.array([[-1.0]] * 8 + [[1.0]] * 8, np.float32)
        spbias = np.concatenate([np.zeros((8, 1), np.float32),
                                 dtb[hg * 8:hg * 8 + 8, None]], axis=0)
        alog8 = np.concatenate([np.zeros((8, 1), np.float32),
                                Alog[hg * 8:hg * 8 + 8, None]], axis=0)
        sl = slice(hg * 1024, hg * 1024 + 1024)
        in_maps.append(dict(
            x=xfull[b], whead=np.ascontiguousarray(whead),
            convw=conv_flat, wab=wab, spsign=spsign,
            spbias=np.ascontiguousarray(spbias), alog=alog8, nw=nwt,
            identf=identf, ibf=ibf, onesf=onesf, e127=e127,
            maskm=maskm, maska=maska, epsc=epsc,
            xh=np.ascontiguousarray(xfull[b, sl]), wout=Wout,
            wgu=wgu, wdn=wdnT, gus=gus, dns=dns))

    nc, _ = build_fused()
    res = run_bass_kernel_spmd(nc, in_maps, core_ids=list(range(8)))

    out = np.empty((B, T, D), np.float32)
    for c in range(8):
        b, th = c // 2, c % 2
        out[b, th * 1024:(th + 1) * 1024] = np.asarray(res.results[c]["out"],
                                                       np.float32)
    return out
